# version 02
# baseline (speedup 1.0000x reference)
import numpy as np
import ml_dtypes
import concourse.bass as bass
import concourse.mybir as mybir
from concourse.bass_types import AP
from concourse.bass_utils import run_bass_kernel_spmd
from contextlib import ExitStack

F32 = mybir.dt.float32
BF16 = mybir.dt.bfloat16
ALU = mybir.AluOpType
ACTF = mybir.ActivationFunctionType
AX = mybir.AxisListType.X

NCORES = 8
B, T, HD = 2, 4096, 2048
TLOC = 1024
NT = TLOC // 128
NEC = HD // 128
NH, DH = 16, 128
E, INTER = 32, 768
NIC = INTER // 128
CAP = 4096.0
NJT = T // 128
SCALING = float(DH) ** -0.5

CONSTS = [1.0, 0.0, -1.0, 0.5, CAP, 1.0 / HD, 1e-5, SCALING]
ENGS = ("sync", "gpsimd", "vector", "scalar", "tensor")

ABF = 37888
AF32 = 2048


class Prog:
    def __init__(self, nc, ctx):
        self.nc = nc
        self.cmds = {e: [] for e in ENGS}
        self.sems = {}
        self.ctx = ctx

    def sem(self, name):
        h = self.ctx.enter_context(self.nc.semaphore(name))
        self.sems[name] = [h, 0]

    def emit(self, eng, fn):
        self.cmds[eng].append(fn)

    def sig(self, eng, sem, fn, k=1):
        rec = self.sems[sem]
        rec[1] += k
        val, h = rec[1], rec[0]

        def cmd(e, fn=fn, h=h, k=k):
            fn(e).then_inc(h, k)
        self.cmds[eng].append(cmd)
        return (sem, val)

    def dma(self, eng, sem, fn):
        return self.sig(eng, sem, fn, 16)

    def val(self, sem):
        return (sem, self.sems[sem][1])

    def wait(self, eng, ev):
        sem, v = ev
        if v <= 0:
            return
        h = self.sems[sem][0]
        self.cmds[eng].append(lambda e, h=h, v=v: e.wait_ge(h, v))

    def wait_now(self, eng, sem):
        self.wait(eng, self.val(sem))

    def play(self):
        with self.nc.Block() as blk:
            for name in ENGS:
                cmds = self.cmds[name]

                def runner(eng, cmds=cmds):
                    for c in cmds:
                        c(eng)
                getattr(blk, name)(runner)
        self.cmds = {e: [] for e in ENGS}


def build_program():
    nc = bass.Bass("TRN2", target_bir_lowering=False)

    x_in = nc.dram_tensor("x_in", [TLOC, HD], F32, kind="ExternalInput")
    out_d = nc.dram_tensor("out", [TLOC, HD], F32, kind="ExternalOutput")
    wq_d = nc.dram_tensor("wq", [128, NEC * HD], F32, kind="ExternalInput")
    wk_d = nc.dram_tensor("wk", [128, NEC * HD], F32, kind="ExternalInput")
    wv_d = nc.dram_tensor("wv", [128, NEC * HD], F32, kind="ExternalInput")
    wo_d = nc.dram_tensor("wo", [128, NEC * HD], F32, kind="ExternalInput")
    bq_d = nc.dram_tensor("bq", [HD], F32, kind="ExternalInput")
    bk_d = nc.dram_tensor("bk", [HD], F32, kind="ExternalInput")
    bv_d = nc.dram_tensor("bv", [HD], F32, kind="ExternalInput")
    bo_d = nc.dram_tensor("bo", [HD], F32, kind="ExternalInput")
    slopes_d = nc.dram_tensor("slopes", [1, NH], F32, kind="ExternalInput")
    gate_d = nc.dram_tensor("gate_w", [128, NEC, E], F32, kind="ExternalInput")
    wgate_d = nc.dram_tensor("wgate", [E, 128, NEC * INTER], F32, kind="ExternalInput")
    wup_d = nc.dram_tensor("wup", [E, 128, NEC * INTER], F32, kind="ExternalInput")
    wdown_d = nc.dram_tensor("wdown", [E, 128, NIC * HD], F32, kind="ExternalInput")
    tri_d = nc.dram_tensor("tri", [128, 128], BF16, kind="ExternalInput")
    ident_d = nc.dram_tensor("ident", [128, 128], BF16, kind="ExternalInput")
    dmat_d = nc.dram_tensor("dmat", [128, 512], F32, kind="ExternalInput")
    ccols_d = nc.dram_tensor("ccols", [128, NJT * 2], F32, kind="ExternalInput")
    corem_d = nc.dram_tensor("corem", [8, 1], F32, kind="ExternalInput")
    xln_dr = nc.dram_tensor("xln_d", [TLOC, HD], BF16)
    q_dr = nc.dram_tensor("q_d", [TLOC, HD], BF16)
    k_dr = nc.dram_tensor("k_d", [TLOC, HD], BF16)
    v_dr = nc.dram_tensor("v_d", [NH, TLOC, DH + 1], BF16)
    attn_dr = nc.dram_tensor("attn_d", [TLOC, HD], BF16)
    kall_dr = nc.dram_tensor("k_all", [T, HD], BF16)
    vall_dr = nc.dram_tensor("v_all", [4 * NH, TLOC, DH + 1], BF16)
    cntin_dr = nc.dram_tensor("cnt_in", [1, E], F32)
    cntout_dr = nc.dram_tensor("cnt_out", [8, E], F32)

    ctx = ExitStack()
    sbt = lambda n, s, d=F32: ctx.enter_context(nc.sbuf_tensor(n, s, d))
    p = Prog(nc, ctx)
    for s in ("ld", "gq", "vt", "tm", "sc", "st", "ccs", "wgl", "wul", "wdl"):
        p.sem(s)

    cst = sbt("cst", [128, len(CONSTS)])
    CIDX = {v: i for i, v in enumerate(CONSTS)}
    C = lambda v: cst[:, CIDX[v]:CIDX[v] + 1]

    x_sb = sbt("x_sb", [128, NT, HD])
    xlnT = sbt("xlnT", [128, NEC, TLOC], BF16)
    tri_sb = sbt("tri_sb", [128, 128], BF16)
    id_sb = sbt("id_sb", [128, 128], BF16)
    dmat_sb = sbt("dmat_sb", [128, 512])
    ccols_sb = sbt("ccols_sb", [128, NJT * 2])
    corem_sb = sbt("corem_sb", [8, 1], BF16)
    ones_r = sbt("ones_r", [1, 128], BF16)
    ones_c = sbt("ones_c", [128, 1], BF16)
    gatew_sb = sbt("gatew_sb", [128, NEC, E], BF16)
    slopes_sb = sbt("slopes_sb", [1, NH])
    slopes_bf = sbt("slopes_bf", [1, NH], BF16)
    negslope = sbt("negslope", [128, NH])
    bias_bf = {m: sbt(f"biasbf_{m}", [1, HD], BF16) for m in "qkvo"}
    cmat = sbt("cmat", [128, NT, E])
    lgt = sbt("lgt", [128, NT, E])
    mxf = sbt("mxf", [128, NT])
    nmxf = sbt("nmxf", [128, NT])
    exf = sbt("exf", [128, NT, E])
    smf = sbt("smf", [128, NT])
    rsmf = sbt("rsmf", [128, NT])
    prf = sbt("prf", [128, E])
    w8f = sbt("w8f", [128, 8])
    denf = sbt("denf", [128, 1])
    rdenf = sbt("rdenf", [128, 1])
    smask = sbt("smask", [128, NT, E])
    smask_bf = sbt("smask_bf", [128, NT, E], BF16)
    cum = sbt("cum", [128, NT, E])
    off_row = sbt("off_row", [1, E], BF16)
    cnt_f = sbt("cnt_f", [1, E])
    agcnt = sbt("agcnt", [8, E], BF16)
    goff_row = sbt("goff_row", [1, E], BF16)
    capv = sbt("capv", [128, E])
    keepm = sbt("keepm", [128, E])
    mcol = sbt("mcol", [128, 1])
    ssq = sbt("ssq", [128, 1])
    stdc = sbt("stdc", [128, 1])
    rstd = sbt("rstd", [128, 1])
    cmstrip = sbt("cmstrip", [128, 2 * NJT])
    colM = sbt("colM", [128, 1])
    colM_bf = sbt("colM_bf", [128, 1], BF16)
    msr = sbt("msr", [1, 1], BF16)
    negM = sbt("negM", [128, 1])
    scol = sbt("scol", [128, 1])
    rscol = sbt("rscol", [128, 1])
    attn_scr = sbt("attn_scr", [128, 2, DH], BF16)

    arena_bf = sbt("arena_bf", [128, ABF], BF16)
    arena_f32 = sbt("arena_f32", [128, AF32], F32)

    def bfv(off, dims):
        return AP(arena_bf, off, [[ABF, 128]] + dims)

    def fv(off, dims):
        return AP(arena_f32, off, [[AF32, 128]] + dims)

    psA = ctx.enter_context(nc.psum_tensor("psA", [128, 2048], F32))
    psB = ctx.enter_context(nc.psum_tensor("psB", [128, 1024], F32))
    psC = ctx.enter_context(nc.psum_tensor("psC", [128, 512], F32))
    ps_tr = ctx.enter_context(nc.psum_tensor("ps_tr", [1, 128], BF16))

    for tt in range(NT):
        p.dma("sync", "ld", lambda s, tt=tt: s.dma_start(out=x_sb[:, tt, :], in_=x_in[tt * 128:(tt + 1) * 128, :]))
    p.dma("sync", "ld", lambda s: s.dma_start(out=tri_sb[:, :], in_=tri_d[:, :]))
    p.dma("sync", "ld", lambda s: s.dma_start(out=id_sb[:, :], in_=ident_d[:, :]))
    p.dma("sync", "ld", lambda s: s.dma_start(out=dmat_sb[:, :], in_=dmat_d[:, :]))
    p.dma("sync", "ld", lambda s: s.dma_start(out=ccols_sb[:, :], in_=ccols_d[:, :]))
    p.dma("sync", "ld", lambda s: s.dma_start(out=slopes_sb[:, :], in_=slopes_d[:, :]))
    for m, bsrc in (("q", bq_d), ("k", bk_d), ("v", bv_d), ("o", bo_d)):
        p.dma("gpsimd", "gq", lambda g, m=m, bsrc=bsrc: g.dma_start(out=bias_bf[m][:1, :], in_=bsrc[None, :]))
    p.dma("gpsimd", "gq", lambda g: g.dma_start(out=gatew_sb[:, :, :], in_=gate_d[:, :, :]))
    p.dma("gpsimd", "gq", lambda g: g.dma_start(out=corem_sb[:, :], in_=corem_d[:, :]))

    for i, val in enumerate(CONSTS):
        p.emit("vector", lambda v, i=i, val=val: v.memset(cst[:, i:i + 1], val))
    p.emit("vector", lambda v: v.memset(ones_r[:, :], 1.0))
    p.emit("vector", lambda v: v.memset(ones_c[:, :], 1.0))
    p.wait_now("vector", "ld")
    p.wait_now("vector", "gq")
    p.sig("vector", "vt", lambda v: v.tensor_copy(slopes_bf[:1, :], slopes_sb[:1, :]))
    p.wait_now("tensor", "vt")
    p.sig("tensor", "tm", lambda t: t.matmul(psC[:, 0:NH], ones_r[:1, :], slopes_bf[:1, :], start=True, stop=True))
    p.wait_now("vector", "tm")
    p.sig("vector", "vt", lambda v: v.tensor_scalar(negslope[:, :], psC[:, 0:NH], C(-1.0), None, op0=ALU.mult))
    p.play()

    def emit_ln(tag):
        xc_t = fv(0, [[1, HD]])
        sqv = bfv(0, [[1, HD]])
        pp = lambda par: bfv(2048 + par * HD, [[1, HD]])
        p.wait_now("vector", "st")
        p.wait_now("vector", "tm")
        stev = {}
        for tt in range(NT):
            ev = p.sig("vector", "vt", lambda v, tt=tt: v.reduce_sum(mcol[:, :], x_sb[:, tt, :], axis=AX))
            p.wait("vector", ev)
            ev = p.sig("vector", "vt", lambda v: v.tensor_scalar(mcol[:, :], mcol[:, :], C(1.0 / HD), None, op0=ALU.mult))
            p.wait("vector", ev)
            p.emit("vector", lambda v, tt=tt: v.tensor_tensor(xc_t, x_sb[:, tt, :], mcol[:, 0:1].to_broadcast([128, HD]), op=ALU.subtract))
            ev_sq = p.sig("vector", "vt", lambda v: v.scalar_tensor_tensor(sqv, xc_t, C(1.0), xc_t, op0=ALU.mult, op1=ALU.mult, accum_out=ssq[:, :]))
            p.wait("scalar", ev_sq)
            ev_std = p.sig("scalar", "sc", lambda s: s.activation(out=stdc[:, :], in_=ssq[:, :], func=ACTF.Sqrt, bias=C(1e-5), scale=1.0 / HD))
            p.wait("vector", ev_std)
            ev = p.sig("vector", "vt", lambda v: v.reciprocal(rstd[:, :], stdc[:, :]))
            p.wait("vector", ev)
            if tt >= 2:
                p.wait("vector", stev[tt - 2])
            ev_x = p.sig("vector", "vt", lambda v, tt=tt: v.tensor_tensor(pp(tt % 2), xc_t, rstd[:, 0:1].to_broadcast([128, HD]), op=ALU.mult))
            p.wait("sync", ev_x)
            stev[tt] = p.dma("sync", "st", lambda s, tt=tt: s.dma_start(out=xln_dr[tt * 128:(tt + 1) * 128, :], in_=pp(tt % 2)))
        p.wait_now("sync", "st")
        p.wait_now("sync", "tm")
        for ec in range(NEC):
            p.dma("sync", "ld", lambda s, ec=ec: s.dma_start(out=xlnT[:, ec, :], in_=xln_dr[:, ec * 128:(ec + 1) * 128], transpose=True))
        p.play()

    def emit_routing(tag):
        ps_lg = [psA[:, 0:E], psA[:, 512:512 + E]]
        ps_cum = psA[:, 1024:1024 + E]
        ps_off = psA[:1, 1536:1536 + E]
        ps_goff = psB[:1, 0:E]
        ps_gbc = psB[:, 512:512 + E]

        p.wait_now("tensor", "ld")
        p.wait_now("tensor", "gq")
        evac_evs = []
        for tt in range(NT):
            if tt >= 2:
                p.wait("tensor", evac_evs[tt - 2])

            def gmm(t, tt=tt, pl=ps_lg[tt % 2]):
                for ec in range(NEC):
                    ins = t.matmul(pl, xlnT[:, ec, tt * 128:(tt + 1) * 128], gatew_sb[:, ec, :],
                                   start=(ec == 0), stop=(ec == NEC - 1))
                return ins
            ev_mm = p.sig("tensor", "tm", gmm)
            p.wait("vector", ev_mm)
            evac_evs.append(p.sig("vector", "vt", lambda v, tt=tt, pl=ps_lg[tt % 2]: v.tensor_copy(lgt[:, tt, :], pl)))

        for tt in range(NT):
            ev_red = p.sig("vector", "vt", lambda v, tt=tt: v.reduce_max(mxf[:, tt:tt + 1], lgt[:, tt, :], axis=AX))
        p.wait("vector", ev_red)
        ev_nm = p.sig("vector", "vt", lambda v: v.tensor_scalar(nmxf[:, :], mxf[:, :], C(-1.0), None, op0=ALU.mult))
        p.wait("scalar", ev_nm)
        for tt in range(NT):
            ev_ex = p.sig("scalar", "sc", lambda s, tt=tt: s.activation(out=exf[:, tt, :], in_=lgt[:, tt, :], func=ACTF.Exp, bias=nmxf[:, tt:tt + 1], scale=1.0, accum_out=smf[:, tt:tt + 1]))
        p.wait("vector", ev_ex)
        ev = p.sig("vector", "vt", lambda v: v.reciprocal(rsmf[:, :], smf[:, :]))
        p.wait("vector", ev)
        for tt in range(NT):
            ev = p.sig("vector", "vt", lambda v, tt=tt: v.tensor_tensor(prf[:, :], exf[:, tt, :], rsmf[:, tt:tt + 1].to_broadcast([128, E]), op=ALU.mult))
            p.wait("vector", ev)
            ev = p.sig("vector", "vt", lambda v: v.max(out=w8f[:, :], in_=prf[:, :]))
            p.wait("vector", ev)
            p.emit("vector", lambda v, tt=tt: v.tensor_scalar(smask[:, tt, :], prf[:, :], w8f[:, 7:8], None, op0=ALU.is_ge))
            ev = p.sig("vector", "vt", lambda v: v.reduce_sum(denf[:, :], w8f[:, :], axis=AX))
            p.wait("vector", ev)
            ev = p.sig("vector", "vt", lambda v: v.reciprocal(rdenf[:, :], denf[:, :]))
            p.wait("vector", ev)
            ev = p.sig("vector", "vt", lambda v: v.tensor_scalar(rdenf[:, :], rdenf[:, :], C(0.5), None, op0=ALU.mult))
            p.wait("vector", ev)
            p.emit("vector", lambda v, tt=tt: v.tensor_tensor(cmat[:, tt, :], prf[:, :], rdenf[:, 0:1].to_broadcast([128, E]), op=ALU.mult))
            p.emit("vector", lambda v, tt=tt: v.tensor_mul(cmat[:, tt, :], cmat[:, tt, :], smask[:, tt, :]))
        p.sig("vector", "vt", lambda v: v.memset(off_row[:1, :], 0.0))
        ev_sm = p.sig("vector", "vt", lambda v: v.tensor_copy(smask_bf[:, :, :], smask[:, :, :]))

        p.wait("tensor", ev_sm)
        off_evs = [p.val("vt")]
        for tt in range(NT):
            if tt > 0:
                p.wait("tensor", off_evs[tt])

            def cmm(t, tt=tt):
                t.matmul(ps_cum, tri_sb[:, :], smask_bf[:, tt, :], start=True, stop=False)
                t.matmul(ps_cum, ones_r[:1, :], off_row[:1, :], start=False, stop=True)
                return t.matmul(ps_off, ones_c[:, :1], smask_bf[:, tt, :], start=(tt == 0), stop=(tt == NT - 1))
            ev_mm = p.sig("tensor", "tm", cmm)
            p.wait("vector", ev_mm)
            p.sig("vector", "vt", lambda v, tt=tt: v.tensor_copy(cum[:, tt, :], ps_cum))
            off_evs.append(p.sig("vector", "vt", lambda v: v.tensor_copy(off_row[:1, :], ps_off)))
        ev_cnt = p.sig("vector", "vt", lambda v: v.tensor_copy(cnt_f[:1, :], ps_off))

        p.wait("gpsimd", ev_cnt)
        ev = p.dma("gpsimd", "st", lambda g: g.dma_start(out=cntin_dr[:, :], in_=cnt_f[:1, :]))
        p.wait("gpsimd", ev)
        ev = p.sig("gpsimd", "ccs", lambda g: g.collective_compute(
            "AllGather", ALU.bypass, replica_groups=[[0, 1, 2, 3, 4, 5, 6, 7]],
            ins=[cntin_dr.ap().opt()], outs=[cntout_dr.ap().opt()]))
        p.wait("gpsimd", ev)
        ev_ag = p.dma("gpsimd", "gq", lambda g: g.dma_start(out=agcnt[:, :], in_=cntout_dr[:, :]))
        p.wait("tensor", ev_ag)
        ev = p.sig("tensor", "tm", lambda t: t.matmul(ps_goff, corem_sb[:, :1], agcnt[:, :], start=True, stop=True))
        p.wait("vector", ev)
        ev = p.sig("vector", "vt", lambda v: v.tensor_copy(goff_row[:1, :], ps_goff))
        p.wait("tensor", ev)
        ev = p.sig("tensor", "tm", lambda t: t.matmul(ps_gbc, ones_r[:1, :], goff_row[:1, :], start=True, stop=True))
        p.wait("vector", ev)
        for tt in range(NT):
            ev = p.sig("vector", "vt", lambda v, tt=tt: v.tensor_add(capv[:, :], cum[:, tt, :], ps_gbc))
            p.wait("vector", ev)
            ev = p.sig("vector", "vt", lambda v: v.tensor_scalar(keepm[:, :], capv[:, :], C(CAP), None, op0=ALU.is_lt))
            p.wait("vector", ev)
            ev = p.sig("vector", "vt", lambda v, tt=tt: v.tensor_mul(cmat[:, tt, :], cmat[:, tt, :], keepm[:, :]))
            p.wait("vector", ev)
        p.play()

    def emit_experts(tag):
        WG, WU, WD, GI, HS = 0, 12288, 24576, 30720, 31744
        wgv = lambda ec, ic: bfv(WG + ec * INTER + ic * 128, [[1, 128]])
        wuv = lambda ec, ic: bfv(WU + ec * INTER + ic * 128, [[1, 128]])
        wdv = lambda slot, ic: bfv(WD + slot * 3072 + ic * 512, [[1, 512]])
        giv = bfv(GI, [[1, TLOC]])
        hv = lambda ic: bfv(HS + ic * TLOC, [[1, TLOC]])
        hts = lambda ic, tt: bfv(HS + ic * TLOC + tt * 128, [[1, 128]])
        ps_g = [psA[:, 0:1024], psA[:, 1024:2048]]
        ps_y = psB[:, 0:512]

        g_fr, u_fr = {}, {}
        g_ld, u_ld, d_ld = {}, {}, {}
        silu_ev, hmul_ev = {}, {}
        dq_fr = {}
        p.wait_now("tensor", "vt")
        p.wait_now("gpsimd", "st")
        p.wait_now("gpsimd", "tm")
        p.wait_now("gpsimd", "vt")
        last_x = None
        for e in range(E):
            if e >= 1:
                p.wait("gpsimd", g_fr[e - 1])
            g_ld[e] = p.dma("gpsimd", "wgl", lambda g, e=e: g.dma_start(
                out=bfv(WG, [[1, NEC * INTER]]), in_=wgate_d[e, :, :]))
            if e >= 1:
                p.wait("gpsimd", u_fr[e - 1])
            u_ld[e] = p.dma("gpsimd", "wul", lambda g, e=e: g.dma_start(
                out=bfv(WU, [[1, NEC * INTER]]), in_=wup_d[e, :, :]))
            if e >= 1:
                p.wait("gpsimd", dq_fr[(e - 1, 2)])
            d_ld[(e, 0)] = p.dma("gpsimd", "wdl", lambda g, e=e: g.dma_start(
                out=bfv(WD, [[512, NIC], [1, 512]]),
                in_=AP(wdown_d, e * 128 * NIC * HD, [[NIC * HD, 128], [HD, NIC], [1, 512]])))

            p.wait("tensor", g_ld[e])
            p.wait("tensor", u_ld[e])
            for ic in range(NIC):
                if (e, ic - 2) in hmul_ev:
                    p.wait("tensor", hmul_ev[(e, ic - 2)])
                elif e > 0 and ic < 2:
                    p.wait("tensor", hmul_ev[(e - 1, NIC - 2 + ic)])

                def gmm(t, ic=ic, ps=ps_g[ic % 2]):
                    for ec in range(NEC):
                        w = wgv(ec, ic)
                        t.matmul(ps[:, 0:512], w, xlnT[:, ec, 0:512], start=(ec == 0), stop=(ec == NEC - 1))
                        ins = t.matmul(ps[:, 512:1024], w, xlnT[:, ec, 512:1024], start=(ec == 0), stop=(ec == NEC - 1))
                    return ins
                ev_g = p.sig("tensor", "tm", gmm)
                p.wait("scalar", ev_g)
                if (e, ic - 1) in hmul_ev:
                    p.wait("scalar", hmul_ev[(e, ic - 1)])
                elif e > 0 and ic == 0:
                    p.wait("scalar", hmul_ev[(e - 1, NIC - 1)])
                silu_ev[(e, ic)] = p.sig("scalar", "sc", lambda s, ps=ps_g[ic % 2]: s.activation(out=giv, in_=ps[:, :], func=ACTF.Silu))
                p.wait("tensor", silu_ev[(e, ic)])

                def umm(t, ic=ic, ps=ps_g[ic % 2]):
                    for ec in range(NEC):
                        w = wuv(ec, ic)
                        t.matmul(ps[:, 0:512], w, xlnT[:, ec, 0:512], start=(ec == 0), stop=(ec == NEC - 1))
                        ins = t.matmul(ps[:, 512:1024], w, xlnT[:, ec, 512:1024], start=(ec == 0), stop=(ec == NEC - 1))
                    return ins
                ev_u = p.sig("tensor", "tm", umm)
                p.wait("vector", ev_u)
                p.wait("vector", silu_ev[(e, ic)])
                hmul_ev[(e, ic)] = p.sig("vector", "vt", lambda v, ic=ic, ps=ps_g[ic % 2]: v.tensor_tensor(hv(ic), ps[:, :], giv, op=ALU.mult))
            g_fr[e] = p.val("tm")
            u_fr[e] = p.val("tm")

            p.wait("tensor", hmul_ev[(e, NIC - 1)])
            for q in range(4):
                slot = q % 2
                if q < 3:
                    if (e, q - 1) in dq_fr:
                        p.wait("gpsimd", dq_fr[(e, q - 1)])
                    elif q == 0 and e >= 1:
                        p.wait("gpsimd", dq_fr[(e - 1, 3)])
                    d_ld[(e, q + 1)] = p.dma("gpsimd", "wdl", lambda g, e=e, q=q: g.dma_start(
                        out=bfv(WD + ((q + 1) % 2) * 3072, [[512, NIC], [1, 512]]),
                        in_=AP(wdown_d, e * 128 * NIC * HD + (q + 1) * 512, [[NIC * HD, 128], [HD, NIC], [1, 512]])))
                p.wait("tensor", d_ld[(e, q)])
                for tt in range(NT):
                    if last_x is not None:
                        p.wait("tensor", last_x)

                    def dmm(t, tt=tt, q=q, slot=slot):
                        for ic in range(NIC):
                            ins = t.matmul(ps_y, hts(ic, tt), wdv(slot, ic),
                                           start=(ic == 0), stop=(ic == NIC - 1))
                        return ins
                    ev_y = p.sig("tensor", "tm", dmm)
                    p.wait("vector", ev_y)

                    def xacc(v, tt=tt, q=q, e=e):
                        return v.scalar_tensor_tensor(
                            x_sb[:, tt, q * 512:(q + 1) * 512], ps_y,
                            cmat[:, tt, e:e + 1],
                            x_sb[:, tt, q * 512:(q + 1) * 512],
                            op0=ALU.mult, op1=ALU.add)
                    last_x = p.sig("vector", "vt", xacc)
                dq_fr[(e, q)] = p.val("tm")
        p.wait_now("vector", "vt")
        p.play()

    emit_ln("a")
    emit_routing("a")
    emit_experts("a")

    emit_ln("b")

    WM, QS, VS = 0, 16384, 20480
    wmv = lambda ec, fc: bfv(WM + ec * 1024 + fc * 512, [[1, 512]])
    qsv = lambda par, fh: bfv(QS + par * HD + fh * 1024, [[1, 1024]])
    vsv = lambda par, fh: bfv(VS + par * NH * (DH + 1) + fh * 8 * (DH + 1), [[DH + 1, 8], [1, DH]])
    ps_m = [psA[:, 0:1024], psA[:, 1024:2048]]
    p.wait_now("vector", "st")
    p.wait_now("vector", "tm")
    p.sig("vector", "vt", lambda v: v.memset(bfv(VS, [[1, 2 * NH * (DH + 1)]]), 1.0))
    for mi, (w_dram, m, dst) in enumerate(((wq_d, "q", q_dr), (wk_d, "k", k_dr), (wv_d, "v", v_dr))):
        ev_evac, ev_store = {}, {}
        for fh in range(2):
            p.wait_now("gpsimd", "tm")
            p.wait_now("gpsimd", "st")
            p.wait_now("gpsimd", "vt")
            ev_w = p.dma("gpsimd", "gq", lambda g, w_dram=w_dram, fh=fh: g.dma_start(
                out=bfv(WM, [[1, NEC * 1024]]),
                in_=AP(w_dram, fh * 1024, [[NEC * HD, 128], [HD, NEC], [1, 1024]])))
            p.wait("tensor", ev_w)
            p.wait_now("tensor", "ld")
            p.wait_now("tensor", "vt")
            for tt in range(NT):
                g2 = tt
                if g2 >= 2:
                    p.wait("tensor", ev_evac[(fh, g2 - 2)])

                def qmm(t, tt=tt, fh=fh, ps=ps_m[tt % 2], m=m):
                    for ec in range(NEC):
                        for fc in range(2):
                            t.matmul(ps[:, fc * 512:(fc + 1) * 512], xlnT[:, ec, tt * 128:(tt + 1) * 128],
                                     wmv(ec, fc), start=(ec == 0), stop=False)
                    for fc in range(2):
                        ins = t.matmul(ps[:, fc * 512:(fc + 1) * 512], ones_r[:1, 0:128],
                                       bias_bf[m][:1, fh * 1024 + fc * 512:fh * 1024 + (fc + 1) * 512],
                                       start=False, stop=True)
                    return ins
                ev_mm = p.sig("tensor", "tm", qmm)
                p.wait("vector", ev_mm)
                if g2 >= 2:
                    p.wait("vector", ev_store[(fh, g2 - 2)])
                elif fh == 1:
                    p.wait("vector", ev_store[(0, NT - 2 + g2)])
                if m == "v":
                    ev_evac[(fh, g2)] = p.sig("vector", "vt", lambda v, tt=tt, fh=fh, ps=ps_m[tt % 2]: v.tensor_copy(
                        vsv(tt % 2, fh), ps.rearrange("a (h d) -> a h d", h=8)))
                elif m == "q":
                    ev_evac[(fh, g2)] = p.sig("vector", "vt", lambda v, tt=tt, fh=fh, ps=ps_m[tt % 2]: v.tensor_scalar(
                        qsv(tt % 2, fh), ps, C(SCALING), None, op0=ALU.mult))
                else:
                    ev_evac[(fh, g2)] = p.sig("vector", "vt", lambda v, tt=tt, fh=fh, ps=ps_m[tt % 2]: v.tensor_copy(
                        qsv(tt % 2, fh), ps))
                p.wait("sync", ev_evac[(fh, g2)])
                if m == "v":
                    ev_store[(fh, g2)] = p.dma("sync", "st", lambda s, tt=tt, fh=fh: s.dma_start(
                        out=AP(v_dr, tt * 128 * (DH + 1) + fh * 8 * TLOC * (DH + 1),
                               [[DH + 1, 128], [TLOC * (DH + 1), 8], [1, DH + 1]]),
                        in_=AP(arena_bf, VS + (tt % 2) * NH * (DH + 1) + fh * 8 * (DH + 1),
                               [[ABF, 128], [DH + 1, 8], [1, DH + 1]])))
                else:
                    ev_store[(fh, g2)] = p.dma("sync", "st", lambda s, tt=tt, fh=fh, dst=dst: s.dma_start(
                        out=dst[tt * 128:(tt + 1) * 128, fh * 1024:(fh + 1) * 1024], in_=qsv(tt % 2, fh)))
    p.play()

    p.wait_now("gpsimd", "st")
    ev_k = p.sig("gpsimd", "ccs", lambda g: g.collective_compute(
        "AllGather", ALU.bypass, replica_groups=[[0, 1, 2, 3], [4, 5, 6, 7]],
        ins=[k_dr.ap().opt()], outs=[kall_dr.ap().opt()]))
    ev_v = p.sig("gpsimd", "ccs", lambda g: g.collective_compute(
        "AllGather", ALU.bypass, replica_groups=[[0, 1, 2, 3], [4, 5, 6, 7]],
        ins=[v_dr.ap().opt()], outs=[vall_dr.ap().opt()]))

    KH, VH, PR = 0, 4096, 8224
    qT = xlnT
    khl = lambda off, n: bfv(KH + off, [[1, n]])
    vhv = lambda jt: bfv(VH + jt * (DH + 1), [[1, DH + 1]])
    prv = lambda k2, a, n: bfv(PR + k2 * 512 + a, [[1, n]])
    absv = lambda k2: fv(k2 * 512, [[1, 512]])
    sttv = lambda k2: fv(1024 + k2 * 512, [[1, 512]])
    ps_s = [psA[:, 0:512], psA[:, 512:1024]]
    ps_o = [psB[:, 0:DH + 1], psB[:, 512:512 + DH + 1], psA[:, 1024:1024 + DH + 1], psA[:, 1536:1536 + DH + 1]]
    ps_bM = psC[:, 0:1]

    p.wait_now("sync", "st")
    p.wait_now("sync", "tm")
    p.wait_now("sync", "vt")
    for ec in range(NEC):
        p.dma("sync", "ld", lambda s, ec=ec: s.dma_start(out=qT[:, ec, :], in_=q_dr[:, ec * 128:(ec + 1) * 128], transpose=True))
    p.wait("sync", ev_k)
    p.wait("sync", ev_v)

    kv_consumed = {}
    stt_evs, exp_evs, pv_evs, evac_evs, evst = {}, {}, {}, {}, {}
    for h in range(NH):
        if h >= 1:
            p.wait("sync", kv_consumed[h - 1])
        for gch in range(8):
            ev_kl = p.dma("sync", "ld", lambda s, h=h, gch=gch: s.dma_start(
                out=khl(gch * 512, 512),
                in_=kall_dr[gch * 512:(gch + 1) * 512, h * 128:(h + 1) * 128], transpose=True))
        for cp in range(4):
            ev_vl = p.dma("sync", "ld", lambda s, h=h, cp=cp: s.dma_start(
                out=AP(arena_bf, VH + cp * 8 * (DH + 1), [[ABF, 128], [DH + 1, 8], [1, DH + 1]]),
                in_=AP(vall_dr, (cp * NH + h) * TLOC * (DH + 1),
                       [[DH + 1, 128], [128 * (DH + 1), 8], [1, DH + 1]])))
        p.wait("tensor", ev_kl)
        p.wait("tensor", ev_vl)
        p.wait_now("tensor", "ld")
        p.wait_now("tensor", "vt")

        red_evs = {}
        for jt in range(NJT):
            for ih in range(2):
                g1 = 2 * jt + ih
                if g1 >= 2:
                    p.wait("tensor", red_evs[g1 - 2])
                ev_mm = p.sig("tensor", "tm", lambda t, h=h, jt=jt, ih=ih, g1=g1: t.matmul(
                    ps_s[g1 % 2], khl(jt * 128, 128),
                    qT[:, h, ih * 512:(ih + 1) * 512], start=True, stop=True))
                p.wait("vector", ev_mm)
                red_evs[g1] = p.sig("vector", "vt", lambda v, g1=g1: v.reduce_max(
                    cmstrip[:, g1:g1 + 1], ps_s[g1 % 2], axis=AX))
        ev = p.sig("vector", "vt", lambda v: v.reduce_max(colM[:, :], cmstrip[:, :], axis=AX))
        p.wait("vector", ev)
        ev = p.sig("vector", "vt", lambda v: v.tensor_copy(colM_bf[:, :], colM[:, :]))
        p.wait("tensor", ev)
        ev = p.sig("tensor", "tm", lambda t: t.transpose(ps_tr[:1, :], colM_bf[:, :], id_sb[:, :]))
        p.wait("vector", ev)
        ev = p.sig("vector", "vt", lambda v: v.reduce_max(msr[:1, :], ps_tr[:1, :], axis=AX))
        p.wait("tensor", ev)
        ev = p.sig("tensor", "tm", lambda t: t.matmul(ps_bM, ones_r[:1, :], msr[:1, :], start=True, stop=True))
        p.wait("vector", ev)
        p.wait_now("vector", "sc")
        ev_negM = p.sig("vector", "vt", lambda v: v.tensor_scalar(negM[:, :], ps_bM, C(-1.0), None, op0=ALU.mult))

        for ic in range(2):
            gidx = h * 2 + ic
            for jt in range(NJT):
                g2 = gidx * NJT + jt
                k2 = jt % 2
                if jt >= 2:
                    p.wait("tensor", stt_evs[g2 - 2])
                if jt < 2:
                    p.wait_now("tensor", "vt")
                ev_mm = p.sig("tensor", "tm", lambda t, h=h, jt=jt, ic=ic, k2=k2: t.matmul(
                    ps_s[k2], khl(jt * 128, 128),
                    qT[:, h, ic * 512:(ic + 1) * 512], start=True, stop=True))
                if g2 >= 2:
                    p.wait("scalar", stt_evs[g2 - 2])
                ev_abs = p.sig("scalar", "sc", lambda s, jt=jt, ic=ic, k2=k2: s.activation(
                    out=absv(k2), in_=dmat_sb[:, :], func=ACTF.Abs,
                    bias=ccols_sb[:, jt * 2 + ic:jt * 2 + ic + 1], scale=1.0))
                p.wait("vector", ev_mm)
                p.wait("vector", ev_abs)
                if g2 >= 2:
                    p.wait("vector", exp_evs[g2 - 2])
                stt_evs[g2] = p.sig("vector", "vt", lambda v, h=h, k2=k2: v.scalar_tensor_tensor(
                    sttv(k2), absv(k2), negslope[:, h:h + 1], ps_s[k2],
                    op0=ALU.mult, op1=ALU.add))
                p.wait("scalar", stt_evs[g2])
                if jt == 0:
                    p.wait("scalar", ev_negM)
                if g2 >= 2:
                    p.wait("scalar", pv_evs[g2 - 2])
                exp_evs[g2] = p.sig("scalar", "sc", lambda s, k2=k2: s.activation(
                    out=prv(k2, 0, 512), in_=sttv(k2), func=ACTF.Exp, bias=negM[:, 0:1], scale=1.0))
                p.wait("tensor", exp_evs[g2])
                if jt == 0 and gidx >= 1:
                    for it in range(4):
                        p.wait("tensor", evac_evs[(gidx - 1, it)])

                def pv(t, h=h, jt=jt, k2=k2):
                    for it in range(4):
                        ins = t.matmul(ps_o[it], prv(k2, it * 128, 128),
                                       vhv(jt), start=(jt == 0), stop=(jt == NJT - 1))
                    return ins
                pv_evs[g2] = p.sig("tensor", "tm", pv)
            for it in range(4):
                gst = gidx * 4 + it
                p.wait("vector", pv_evs[gidx * NJT + NJT - 1])
                ev = p.sig("vector", "vt", lambda v, it=it: v.tensor_copy(scol[:, :], ps_o[it][:, DH:DH + 1]))
                p.wait("vector", ev)
                ev = p.sig("vector", "vt", lambda v: v.reciprocal(rscol[:, :], scol[:, :]))
                p.wait("vector", ev)
                if gst >= 2:
                    p.wait("vector", evst[gst - 2])
                ev2 = p.sig("vector", "vt", lambda v, it=it, gst=gst: v.tensor_tensor(
                    attn_scr[:, gst % 2, :], ps_o[it][:, 0:DH], rscol[:, 0:1].to_broadcast([128, DH]), op=ALU.mult))
                evac_evs[(gidx, it)] = ev2
                p.wait("sync", ev2)
                evst[gst] = p.dma("sync", "st", lambda s, h=h, ic=ic, it=it, gst=gst: s.dma_start(
                    out=attn_dr[ic * 512 + it * 128: ic * 512 + (it + 1) * 128, h * 128:(h + 1) * 128],
                    in_=attn_scr[:, gst % 2, :]))
        kv_consumed[h] = p.val("tm")
    p.play()

    attnT = xlnT
    wov = lambda ec, fc: bfv(ec * 1024 + fc * 512, [[1, 512]])
    ps_w = [psA[:, 0:1024], psA[:, 1024:2048]]
    p.wait_now("sync", "st")
    p.wait_now("sync", "tm")
    for ec in range(NEC):
        p.dma("sync", "ld", lambda s, ec=ec: s.dma_start(out=attnT[:, ec, :], in_=attn_dr[:, ec * 128:(ec + 1) * 128], transpose=True))
    for fh in range(2):
        p.wait_now("gpsimd", "tm")
        ev_w = p.dma("gpsimd", "gq", lambda g, fh=fh: g.dma_start(
            out=bfv(0, [[1, NEC * 1024]]), in_=AP(wo_d, fh * TLOC, [[NEC * HD, 128], [HD, NEC], [1, TLOC]])))
        p.wait("tensor", ev_w)
        p.wait_now("tensor", "ld")
        p.wait_now("tensor", "vt")
        xwo_ev = {}
        for tt in range(NT):
            if tt >= 2:
                p.wait("tensor", xwo_ev[tt - 2])

            def womm(t, tt=tt, fh=fh, ps=ps_w[tt % 2]):
                for ec in range(NEC):
                    for fc in range(2):
                        t.matmul(ps[:, fc * 512:(fc + 1) * 512], attnT[:, ec, tt * 128:(tt + 1) * 128],
                                 wov(ec, fc), start=(ec == 0), stop=False)
                for fc in range(2):
                    ins = t.matmul(ps[:, fc * 512:(fc + 1) * 512], ones_r[:1, 0:128],
                                   bias_bf["o"][:1, fh * TLOC + fc * 512:fh * TLOC + (fc + 1) * 512],
                                   start=False, stop=True)
                return ins
            ev_mm = p.sig("tensor", "tm", womm)
            p.wait("vector", ev_mm)

            def xadd(v, tt=tt, fh=fh, ps=ps_w[tt % 2]):
                for fc in range(2):
                    ins = v.tensor_add(x_sb[:, tt, fh * TLOC + fc * 512:fh * TLOC + (fc + 1) * 512],
                                       x_sb[:, tt, fh * TLOC + fc * 512:fh * TLOC + (fc + 1) * 512],
                                       ps[:, fc * 512:(fc + 1) * 512])
                return ins
            xwo_ev[tt] = p.sig("vector", "vt", xadd)
    p.wait_now("vector", "vt")
    p.play()

    emit_ln("c")
    emit_routing("c")
    emit_experts("c")

    p.wait_now("sync", "vt")
    for tt in range(NT):
        p.dma("sync", "st", lambda s, tt=tt: s.dma_start(out=out_d[tt * 128:(tt + 1) * 128, :], in_=x_sb[:, tt, :]))
    p.wait_now("sync", "st")
    p.play()

    ctx.close()
    return nc


def _prep_weights(inputs):
    def chunkT(w):
        wt = np.ascontiguousarray(w.T)
        return np.ascontiguousarray(wt.reshape(NEC, 128, HD).transpose(1, 0, 2).reshape(128, NEC * HD))

    d = {}
    d["wq"] = chunkT(inputs["L0_attn_Wq"])
    d["wk"] = chunkT(inputs["L0_attn_Wk"])
    d["wv"] = chunkT(inputs["L0_attn_Wv"])
    d["wo"] = chunkT(inputs["L0_attn_Wo"])
    d["bq"] = np.ascontiguousarray(inputs["L0_attn_bq"])
    d["bk"] = np.ascontiguousarray(inputs["L0_attn_bk"])
    d["bv"] = np.ascontiguousarray(inputs["L0_attn_bv"])
    d["bo"] = np.ascontiguousarray(inputs["L0_attn_bo"])
    d["slopes"] = np.ascontiguousarray(inputs["L0_attn_slopes"]).reshape(1, NH)
    d["gate_w"] = np.ascontiguousarray(inputs["L0_ffn_gate_w"].reshape(NEC, 128, E).transpose(1, 0, 2))
    d["wgate"] = np.ascontiguousarray(
        inputs["L0_ffn_w_gate"].reshape(E, NEC, 128, INTER).transpose(0, 2, 1, 3).reshape(E, 128, NEC * INTER))
    d["wup"] = np.ascontiguousarray(
        inputs["L0_ffn_w_up"].reshape(E, NEC, 128, INTER).transpose(0, 2, 1, 3).reshape(E, 128, NEC * INTER))
    d["wdown"] = np.ascontiguousarray(
        inputs["L0_ffn_w_down"].reshape(E, NIC, 128, HD).transpose(0, 2, 1, 3).reshape(E, 128, NIC * HD))
    d["tri"] = np.ascontiguousarray(np.tril(np.ones((128, 128), np.float32), -1).T).astype(ml_dtypes.bfloat16)
    d["ident"] = np.eye(128, dtype=np.float32).astype(ml_dtypes.bfloat16)
    d["dmat"] = np.ascontiguousarray(
        np.arange(512, dtype=np.float32)[None, :] - np.arange(128, dtype=np.float32)[:, None])
    return d


def kernel(L0_attn_x, L0_attn_Wq, L0_attn_bq, L0_attn_Wk, L0_attn_bk, L0_attn_Wv,
           L0_attn_bv, L0_attn_Wo, L0_attn_bo, L0_attn_slopes, L0_ffn_gate_w,
           L0_ffn_w_gate, L0_ffn_w_up, L0_ffn_w_down):
    inputs = dict(L0_attn_x=np.asarray(L0_attn_x), L0_attn_Wq=np.asarray(L0_attn_Wq),
                  L0_attn_bq=np.asarray(L0_attn_bq), L0_attn_Wk=np.asarray(L0_attn_Wk),
                  L0_attn_bk=np.asarray(L0_attn_bk), L0_attn_Wv=np.asarray(L0_attn_Wv),
                  L0_attn_bv=np.asarray(L0_attn_bv), L0_attn_Wo=np.asarray(L0_attn_Wo),
                  L0_attn_bo=np.asarray(L0_attn_bo), L0_attn_slopes=np.asarray(L0_attn_slopes),
                  L0_ffn_gate_w=np.asarray(L0_ffn_gate_w), L0_ffn_w_gate=np.asarray(L0_ffn_w_gate),
                  L0_ffn_w_up=np.asarray(L0_ffn_w_up), L0_ffn_w_down=np.asarray(L0_ffn_w_down))
    shared = _prep_weights(inputs)
    x = inputs["L0_attn_x"]

    nc = build_program()
    in_maps = []
    for c in range(NCORES):
        b, r = c // 4, c % 4
        m = dict(shared)
        m["x_in"] = np.ascontiguousarray(x[b, r * TLOC:(r + 1) * TLOC, :])
        qbase = r * TLOC
        cc = np.zeros((128, NJT * 2), np.float32)
        for jt in range(NJT):
            for ic in range(2):
                cc[:, jt * 2 + ic] = qbase + ic * 512 - jt * 128
        m["ccols"] = cc
        cm = np.zeros((8, 1), np.float32)
        cm[:c] = 1.0
        m["corem"] = cm
        in_maps.append(m)

    res = run_bass_kernel_spmd(nc, in_maps, core_ids=list(range(NCORES)))
    out = np.empty((B, T, HD), np.float32)
    for c in range(NCORES):
        b, r = c // 4, c % 4
        out[b, r * TLOC:(r + 1) * TLOC, :] = res.results[c]["out"]
    return out


# version 03
# speedup vs baseline: 1.0768x; 1.0768x over previous
import numpy as np
import ml_dtypes
import concourse.bass as bass
import concourse.mybir as mybir
from concourse.bass_types import AP
from concourse.bass_utils import run_bass_kernel_spmd
from contextlib import ExitStack

F32 = mybir.dt.float32
BF16 = mybir.dt.bfloat16
ALU = mybir.AluOpType
ACTF = mybir.ActivationFunctionType
AX = mybir.AxisListType.X

NCORES = 8
B, T, HD = 2, 4096, 2048
TLOC = 1024
NT = TLOC // 128
NEC = HD // 128
NH, DH = 16, 128
E, INTER = 32, 768
NIC = INTER // 128
CAP = 4096.0
NJT = T // 128
SCALING = float(DH) ** -0.5

CONSTS = [1.0, 0.0, -1.0, 0.5, CAP, 1.0 / HD, 1e-5, SCALING]
ENGS = ("sync", "gpsimd", "vector", "scalar", "tensor")

ABF = 37888
AF32 = 2048


class Prog:
    def __init__(self, nc, ctx):
        self.nc = nc
        self.cmds = {e: [] for e in ENGS}
        self.sems = {}
        self.ctx = ctx

    def sem(self, name):
        h = self.ctx.enter_context(self.nc.semaphore(name))
        self.sems[name] = [h, 0]

    def emit(self, eng, fn):
        self.cmds[eng].append(fn)

    def sig(self, eng, sem, fn, k=1):
        rec = self.sems[sem]
        rec[1] += k
        val, h = rec[1], rec[0]

        def cmd(e, fn=fn, h=h, k=k):
            fn(e).then_inc(h, k)
        self.cmds[eng].append(cmd)
        return (sem, val)

    def dma(self, eng, sem, fn):
        return self.sig(eng, sem, fn, 16)

    def val(self, sem):
        return (sem, self.sems[sem][1])

    def wait(self, eng, ev):
        sem, v = ev
        if v <= 0:
            return
        h = self.sems[sem][0]
        self.cmds[eng].append(lambda e, h=h, v=v: e.wait_ge(h, v))

    def wait_now(self, eng, sem):
        self.wait(eng, self.val(sem))

    def play(self):
        with self.nc.Block() as blk:
            for name in ENGS:
                cmds = self.cmds[name]

                def runner(eng, cmds=cmds):
                    for c in cmds:
                        c(eng)
                getattr(blk, name)(runner)
        self.cmds = {e: [] for e in ENGS}


def build_program():
    nc = bass.Bass("TRN2", target_bir_lowering=False)

    x_in = nc.dram_tensor("x_in", [TLOC, HD], F32, kind="ExternalInput")
    out_d = nc.dram_tensor("out", [TLOC, HD], F32, kind="ExternalOutput")
    wq_d = nc.dram_tensor("wq", [128, NEC * HD], F32, kind="ExternalInput")
    wk_d = nc.dram_tensor("wk", [128, NEC * HD], F32, kind="ExternalInput")
    wv_d = nc.dram_tensor("wv", [128, NEC * HD], F32, kind="ExternalInput")
    wo_d = nc.dram_tensor("wo", [128, NEC * HD], F32, kind="ExternalInput")
    bq_d = nc.dram_tensor("bq", [HD], F32, kind="ExternalInput")
    bk_d = nc.dram_tensor("bk", [HD], F32, kind="ExternalInput")
    bv_d = nc.dram_tensor("bv", [HD], F32, kind="ExternalInput")
    bo_d = nc.dram_tensor("bo", [HD], F32, kind="ExternalInput")
    slopes_d = nc.dram_tensor("slopes", [1, NH], F32, kind="ExternalInput")
    gate_d = nc.dram_tensor("gate_w", [128, NEC, E], F32, kind="ExternalInput")
    wgate_d = nc.dram_tensor("wgate", [E, 128, NEC * INTER], F32, kind="ExternalInput")
    wup_d = nc.dram_tensor("wup", [E, 128, NEC * INTER], F32, kind="ExternalInput")
    wdown_d = nc.dram_tensor("wdown", [E, 128, NIC * HD], F32, kind="ExternalInput")
    tri_d = nc.dram_tensor("tri", [128, 128], BF16, kind="ExternalInput")
    ident_d = nc.dram_tensor("ident", [128, 128], BF16, kind="ExternalInput")
    absm_d = nc.dram_tensor("absm", [128, 4992], F32, kind="ExternalInput")
    corem_d = nc.dram_tensor("corem", [8, 1], F32, kind="ExternalInput")
    xln_dr = nc.dram_tensor("xln_d", [TLOC, HD], BF16)
    q_dr = nc.dram_tensor("q_d", [TLOC, HD], BF16)
    k_dr = nc.dram_tensor("k_d", [TLOC, HD], BF16)
    v_dr = nc.dram_tensor("v_d", [NH, TLOC, DH + 1], BF16)
    attn_dr = nc.dram_tensor("attn_d", [TLOC, HD], BF16)
    kall_dr = nc.dram_tensor("k_all", [T, HD], BF16)
    vall_dr = nc.dram_tensor("v_all", [4 * NH, TLOC, DH + 1], BF16)
    cntin_dr = nc.dram_tensor("cnt_in", [1, E], F32)
    cntout_dr = nc.dram_tensor("cnt_out", [8, E], F32)

    ctx = ExitStack()
    sbt = lambda n, s, d=F32: ctx.enter_context(nc.sbuf_tensor(n, s, d))
    p = Prog(nc, ctx)
    for s in ("ld", "gq", "vt", "tm", "sc", "st", "ccs", "wgl", "wul", "wdl"):
        p.sem(s)

    cst = sbt("cst", [128, len(CONSTS)])
    CIDX = {v: i for i, v in enumerate(CONSTS)}
    C = lambda v: cst[:, CIDX[v]:CIDX[v] + 1]

    x_sb = sbt("x_sb", [128, NT, HD])
    xlnT = sbt("xlnT", [128, NEC, TLOC], BF16)
    tri_sb = sbt("tri_sb", [128, 128], BF16)
    id_sb = sbt("id_sb", [128, 128], BF16)
    corem_sb = sbt("corem_sb", [8, 1], BF16)
    ones_r = sbt("ones_r", [1, 128], BF16)
    ones_c = sbt("ones_c", [128, 1], BF16)
    gatew_sb = sbt("gatew_sb", [128, NEC, E], BF16)
    slopes_sb = sbt("slopes_sb", [1, NH])
    slopes_bf = sbt("slopes_bf", [1, NH], BF16)
    negslope = sbt("negslope", [128, NH])
    bias_bf = {m: sbt(f"biasbf_{m}", [1, HD], BF16) for m in "qkvo"}
    cmat = sbt("cmat", [128, NT, E])
    lgt = sbt("lgt", [128, NT, E])
    mxf = sbt("mxf", [128, NT])
    nmxf = sbt("nmxf", [128, NT])
    exf = sbt("exf", [128, NT, E])
    smf = sbt("smf", [128, NT])
    rsmf = sbt("rsmf", [128, NT])
    prf = sbt("prf", [128, E])
    w8f = sbt("w8f", [128, 8])
    denf = sbt("denf", [128, 1])
    rdenf = sbt("rdenf", [128, 1])
    smask = sbt("smask", [128, NT, E])
    smask_bf = sbt("smask_bf", [128, NT, E], BF16)
    cum = sbt("cum", [128, NT, E])
    off_row = sbt("off_row", [1, E], BF16)
    cnt_f = sbt("cnt_f", [1, E])
    agcnt = sbt("agcnt", [8, E], BF16)
    goff_row = sbt("goff_row", [1, E], BF16)
    capv = sbt("capv", [128, E])
    keepm = sbt("keepm", [128, E])
    mcol = sbt("mcol", [128, 1])
    ssq = sbt("ssq", [128, 1])
    stdc = sbt("stdc", [128, 1])
    rstd = sbt("rstd", [128, 1])
    cmstrip = sbt("cmstrip", [128, 2 * NJT])
    colM = sbt("colM", [128, 1])
    colM_bf = sbt("colM_bf", [128, 1], BF16)
    msr = sbt("msr", [1, 1], BF16)
    negM = sbt("negM", [128, 1])
    scol = sbt("scol", [128, 1])
    rscol = sbt("rscol", [128, 1])
    attn_scr = sbt("attn_scr", [128, 2, DH], BF16)

    arena_bf = sbt("arena_bf", [128, ABF], BF16)
    arena_f32 = sbt("arena_f32", [128, AF32], F32)

    def bfv(off, dims):
        return AP(arena_bf, off, [[ABF, 128]] + dims)

    def fv(off, dims):
        return AP(arena_f32, off, [[AF32, 128]] + dims)

    psA = ctx.enter_context(nc.psum_tensor("psA", [128, 2048], F32))
    psB = ctx.enter_context(nc.psum_tensor("psB", [128, 1024], F32))
    psC = ctx.enter_context(nc.psum_tensor("psC", [128, 512], F32))
    ps_tr = ctx.enter_context(nc.psum_tensor("ps_tr", [1, 128], BF16))

    for tt in range(NT):
        p.dma("sync", "ld", lambda s, tt=tt: s.dma_start(out=x_sb[:, tt, :], in_=x_in[tt * 128:(tt + 1) * 128, :]))
    p.dma("sync", "ld", lambda s: s.dma_start(out=tri_sb[:, :], in_=tri_d[:, :]))
    p.dma("sync", "ld", lambda s: s.dma_start(out=id_sb[:, :], in_=ident_d[:, :]))
    p.dma("sync", "ld", lambda s: s.dma_start(out=slopes_sb[:, :], in_=slopes_d[:, :]))
    for m, bsrc in (("q", bq_d), ("k", bk_d), ("v", bv_d), ("o", bo_d)):
        p.dma("gpsimd", "gq", lambda g, m=m, bsrc=bsrc: g.dma_start(out=bias_bf[m][:1, :], in_=bsrc[None, :]))
    p.dma("gpsimd", "gq", lambda g: g.dma_start(out=gatew_sb[:, :, :], in_=gate_d[:, :, :]))
    p.dma("gpsimd", "gq", lambda g: g.dma_start(out=corem_sb[:, :], in_=corem_d[:, :]))

    for i, val in enumerate(CONSTS):
        p.emit("vector", lambda v, i=i, val=val: v.memset(cst[:, i:i + 1], val))
    p.emit("vector", lambda v: v.memset(ones_r[:, :], 1.0))
    p.emit("vector", lambda v: v.memset(ones_c[:, :], 1.0))
    p.wait_now("vector", "ld")
    p.wait_now("vector", "gq")
    p.sig("vector", "vt", lambda v: v.tensor_copy(slopes_bf[:1, :], slopes_sb[:1, :]))
    p.wait_now("tensor", "vt")
    p.sig("tensor", "tm", lambda t: t.matmul(psC[:, 0:NH], ones_r[:1, :], slopes_bf[:1, :], start=True, stop=True))
    p.wait_now("vector", "tm")
    p.sig("vector", "vt", lambda v: v.tensor_scalar(negslope[:, :], psC[:, 0:NH], C(-1.0), None, op0=ALU.mult))
    p.play()

    def emit_ln(tag):
        xc_t = fv(0, [[1, HD]])
        sqv = bfv(0, [[1, HD]])
        pp = lambda par: bfv(2048 + par * HD, [[1, HD]])
        p.wait_now("vector", "st")
        p.wait_now("vector", "tm")
        stev = {}
        for tt in range(NT):
            ev = p.sig("vector", "vt", lambda v, tt=tt: v.reduce_sum(mcol[:, :], x_sb[:, tt, :], axis=AX))
            p.wait("vector", ev)
            ev = p.sig("vector", "vt", lambda v: v.tensor_scalar(mcol[:, :], mcol[:, :], C(1.0 / HD), None, op0=ALU.mult))
            p.wait("vector", ev)
            p.emit("vector", lambda v, tt=tt: v.tensor_tensor(xc_t, x_sb[:, tt, :], mcol[:, 0:1].to_broadcast([128, HD]), op=ALU.subtract))
            ev_sq = p.sig("vector", "vt", lambda v: v.scalar_tensor_tensor(sqv, xc_t, C(1.0), xc_t, op0=ALU.mult, op1=ALU.mult, accum_out=ssq[:, :]))
            p.wait("scalar", ev_sq)
            ev_std = p.sig("scalar", "sc", lambda s: s.activation(out=stdc[:, :], in_=ssq[:, :], func=ACTF.Sqrt, bias=C(1e-5), scale=1.0 / HD))
            p.wait("vector", ev_std)
            ev = p.sig("vector", "vt", lambda v: v.reciprocal(rstd[:, :], stdc[:, :]))
            p.wait("vector", ev)
            if tt >= 2:
                p.wait("vector", stev[tt - 2])
            ev_x = p.sig("vector", "vt", lambda v, tt=tt: v.tensor_tensor(pp(tt % 2), xc_t, rstd[:, 0:1].to_broadcast([128, HD]), op=ALU.mult))
            p.wait("sync", ev_x)
            stev[tt] = p.dma("sync", "st", lambda s, tt=tt: s.dma_start(out=xln_dr[tt * 128:(tt + 1) * 128, :], in_=pp(tt % 2)))
        p.wait_now("sync", "st")
        p.wait_now("sync", "tm")
        for ec in range(NEC):
            p.dma("sync", "ld", lambda s, ec=ec: s.dma_start(out=xlnT[:, ec, :], in_=xln_dr[:, ec * 128:(ec + 1) * 128], transpose=True))
        p.play()

    def emit_routing(tag):
        ps_lg = [psA[:, 0:E], psA[:, 512:512 + E]]
        ps_cum = psA[:, 1024:1024 + E]
        ps_off = psA[:1, 1536:1536 + E]
        ps_goff = psB[:1, 0:E]
        ps_gbc = psB[:, 512:512 + E]

        p.wait_now("tensor", "ld")
        p.wait_now("tensor", "gq")
        evac_evs = []
        for tt in range(NT):
            if tt >= 2:
                p.wait("tensor", evac_evs[tt - 2])

            def gmm(t, tt=tt, pl=ps_lg[tt % 2]):
                for ec in range(NEC):
                    ins = t.matmul(pl, xlnT[:, ec, tt * 128:(tt + 1) * 128], gatew_sb[:, ec, :],
                                   start=(ec == 0), stop=(ec == NEC - 1))
                return ins
            ev_mm = p.sig("tensor", "tm", gmm)
            p.wait("vector", ev_mm)
            evac_evs.append(p.sig("vector", "vt", lambda v, tt=tt, pl=ps_lg[tt % 2]: v.tensor_copy(lgt[:, tt, :], pl)))

        for tt in range(NT):
            ev_red = p.sig("vector", "vt", lambda v, tt=tt: v.reduce_max(mxf[:, tt:tt + 1], lgt[:, tt, :], axis=AX))
        p.wait("vector", ev_red)
        ev_nm = p.sig("vector", "vt", lambda v: v.tensor_scalar(nmxf[:, :], mxf[:, :], C(-1.0), None, op0=ALU.mult))
        p.wait("scalar", ev_nm)
        for tt in range(NT):
            ev_ex = p.sig("scalar", "sc", lambda s, tt=tt: s.activation(out=exf[:, tt, :], in_=lgt[:, tt, :], func=ACTF.Exp, bias=nmxf[:, tt:tt + 1], scale=1.0, accum_out=smf[:, tt:tt + 1]))
        p.wait("vector", ev_ex)
        ev = p.sig("vector", "vt", lambda v: v.reciprocal(rsmf[:, :], smf[:, :]))
        p.wait("vector", ev)
        for tt in range(NT):
            ev = p.sig("vector", "vt", lambda v, tt=tt: v.tensor_tensor(prf[:, :], exf[:, tt, :], rsmf[:, tt:tt + 1].to_broadcast([128, E]), op=ALU.mult))
            p.wait("vector", ev)
            ev = p.sig("vector", "vt", lambda v: v.max(out=w8f[:, :], in_=prf[:, :]))
            p.wait("vector", ev)
            p.emit("vector", lambda v, tt=tt: v.tensor_scalar(smask[:, tt, :], prf[:, :], w8f[:, 7:8], None, op0=ALU.is_ge))
            ev = p.sig("vector", "vt", lambda v: v.reduce_sum(denf[:, :], w8f[:, :], axis=AX))
            p.wait("vector", ev)
            ev = p.sig("vector", "vt", lambda v: v.reciprocal(rdenf[:, :], denf[:, :]))
            p.wait("vector", ev)
            ev = p.sig("vector", "vt", lambda v: v.tensor_scalar(rdenf[:, :], rdenf[:, :], C(0.5), None, op0=ALU.mult))
            p.wait("vector", ev)
            p.emit("vector", lambda v, tt=tt: v.tensor_tensor(cmat[:, tt, :], prf[:, :], rdenf[:, 0:1].to_broadcast([128, E]), op=ALU.mult))
            p.emit("vector", lambda v, tt=tt: v.tensor_mul(cmat[:, tt, :], cmat[:, tt, :], smask[:, tt, :]))
        p.sig("vector", "vt", lambda v: v.memset(off_row[:1, :], 0.0))
        ev_sm = p.sig("vector", "vt", lambda v: v.tensor_copy(smask_bf[:, :, :], smask[:, :, :]))

        p.wait("tensor", ev_sm)
        off_evs = [p.val("vt")]
        for tt in range(NT):
            if tt > 0:
                p.wait("tensor", off_evs[tt])

            def cmm(t, tt=tt):
                t.matmul(ps_cum, tri_sb[:, :], smask_bf[:, tt, :], start=True, stop=False)
                t.matmul(ps_cum, ones_r[:1, :], off_row[:1, :], start=False, stop=True)
                return t.matmul(ps_off, ones_c[:, :1], smask_bf[:, tt, :], start=(tt == 0), stop=(tt == NT - 1))
            ev_mm = p.sig("tensor", "tm", cmm)
            p.wait("vector", ev_mm)
            p.sig("vector", "vt", lambda v, tt=tt: v.tensor_copy(cum[:, tt, :], ps_cum))
            off_evs.append(p.sig("vector", "vt", lambda v: v.tensor_copy(off_row[:1, :], ps_off)))
        ev_cnt = p.sig("vector", "vt", lambda v: v.tensor_copy(cnt_f[:1, :], ps_off))

        p.wait("gpsimd", ev_cnt)
        ev = p.dma("gpsimd", "st", lambda g: g.dma_start(out=cntin_dr[:, :], in_=cnt_f[:1, :]))
        p.wait("gpsimd", ev)
        ev = p.sig("gpsimd", "ccs", lambda g: g.collective_compute(
            "AllGather", ALU.bypass, replica_groups=[[0, 1, 2, 3, 4, 5, 6, 7]],
            ins=[cntin_dr.ap().opt()], outs=[cntout_dr.ap().opt()]))
        p.wait("gpsimd", ev)
        ev_ag = p.dma("gpsimd", "gq", lambda g: g.dma_start(out=agcnt[:, :], in_=cntout_dr[:, :]))
        p.wait("tensor", ev_ag)
        ev = p.sig("tensor", "tm", lambda t: t.matmul(ps_goff, corem_sb[:, :1], agcnt[:, :], start=True, stop=True))
        p.wait("vector", ev)
        ev = p.sig("vector", "vt", lambda v: v.tensor_copy(goff_row[:1, :], ps_goff))
        p.wait("tensor", ev)
        ev = p.sig("tensor", "tm", lambda t: t.matmul(ps_gbc, ones_r[:1, :], goff_row[:1, :], start=True, stop=True))
        p.wait("vector", ev)
        for tt in range(NT):
            ev = p.sig("vector", "vt", lambda v, tt=tt: v.tensor_add(capv[:, :], cum[:, tt, :], ps_gbc))
            p.wait("vector", ev)
            ev = p.sig("vector", "vt", lambda v: v.tensor_scalar(keepm[:, :], capv[:, :], C(CAP), None, op0=ALU.is_lt))
            p.wait("vector", ev)
            ev = p.sig("vector", "vt", lambda v, tt=tt: v.tensor_mul(cmat[:, tt, :], cmat[:, tt, :], keepm[:, :]))
            p.wait("vector", ev)
        p.play()

    def emit_experts(tag):
        WG, WU, WD, GI, HS = 0, 12288, 24576, 30720, 31744
        wgv = lambda ec, ic: bfv(WG + ec * INTER + ic * 128, [[1, 128]])
        wuv = lambda ec, ic: bfv(WU + ec * INTER + ic * 128, [[1, 128]])
        wdv = lambda slot, ic: bfv(WD + slot * 3072 + ic * 512, [[1, 512]])
        giv = bfv(GI, [[1, TLOC]])
        hv = lambda ic: bfv(HS + ic * TLOC, [[1, TLOC]])
        hts = lambda ic, tt: bfv(HS + ic * TLOC + tt * 128, [[1, 128]])
        ps_g = [psA[:, 0:1024], psA[:, 1024:2048]]
        ps_y = [psB[:, 0:512], psB[:, 512:1024]]

        g_fr, u_fr = {}, {}
        g_ld, u_ld, d_ld = {}, {}, {}
        silu_ev, hmul_ev = {}, {}
        dq_fr = {}
        p.wait_now("tensor", "vt")
        p.wait_now("gpsimd", "st")
        p.wait_now("gpsimd", "tm")
        p.wait_now("gpsimd", "vt")
        last_x = None
        xacc_evs = {}
        for e in range(E):
            if e >= 1:
                p.wait("gpsimd", g_fr[e - 1])
            g_ld[e] = p.dma("gpsimd", "wgl", lambda g, e=e: g.dma_start(
                out=bfv(WG, [[1, NEC * INTER]]), in_=wgate_d[e, :, :]))
            if e >= 1:
                p.wait("gpsimd", u_fr[e - 1])
            u_ld[e] = p.dma("gpsimd", "wul", lambda g, e=e: g.dma_start(
                out=bfv(WU, [[1, NEC * INTER]]), in_=wup_d[e, :, :]))
            if e >= 1:
                p.wait("gpsimd", dq_fr[(e - 1, 2)])
            d_ld[(e, 0)] = p.dma("gpsimd", "wdl", lambda g, e=e: g.dma_start(
                out=bfv(WD, [[512, NIC], [1, 512]]),
                in_=AP(wdown_d, e * 128 * NIC * HD, [[NIC * HD, 128], [HD, NIC], [1, 512]])))

            p.wait("tensor", g_ld[e])
            p.wait("tensor", u_ld[e])
            for ic in range(NIC):
                if (e, ic - 2) in hmul_ev:
                    p.wait("tensor", hmul_ev[(e, ic - 2)])
                elif e > 0 and ic < 2:
                    p.wait("tensor", hmul_ev[(e - 1, NIC - 2 + ic)])

                def gmm(t, ic=ic, ps=ps_g[ic % 2]):
                    for ec in range(NEC):
                        w = wgv(ec, ic)
                        t.matmul(ps[:, 0:512], w, xlnT[:, ec, 0:512], start=(ec == 0), stop=(ec == NEC - 1))
                        ins = t.matmul(ps[:, 512:1024], w, xlnT[:, ec, 512:1024], start=(ec == 0), stop=(ec == NEC - 1))
                    return ins
                ev_g = p.sig("tensor", "tm", gmm)
                p.wait("scalar", ev_g)
                if (e, ic - 1) in hmul_ev:
                    p.wait("scalar", hmul_ev[(e, ic - 1)])
                elif e > 0 and ic == 0:
                    p.wait("scalar", hmul_ev[(e - 1, NIC - 1)])
                silu_ev[(e, ic)] = p.sig("scalar", "sc", lambda s, ps=ps_g[ic % 2]: s.activation(out=giv, in_=ps[:, :], func=ACTF.Silu))
                p.wait("tensor", silu_ev[(e, ic)])

                def umm(t, ic=ic, ps=ps_g[ic % 2]):
                    for ec in range(NEC):
                        w = wuv(ec, ic)
                        t.matmul(ps[:, 0:512], w, xlnT[:, ec, 0:512], start=(ec == 0), stop=(ec == NEC - 1))
                        ins = t.matmul(ps[:, 512:1024], w, xlnT[:, ec, 512:1024], start=(ec == 0), stop=(ec == NEC - 1))
                    return ins
                ev_u = p.sig("tensor", "tm", umm)
                p.wait("vector", ev_u)
                p.wait("vector", silu_ev[(e, ic)])
                hmul_ev[(e, ic)] = p.sig("vector", "vt", lambda v, ic=ic, ps=ps_g[ic % 2]: v.tensor_tensor(hv(ic), ps[:, :], giv, op=ALU.mult))
            g_fr[e] = p.val("tm")
            u_fr[e] = p.val("tm")

            p.wait("tensor", hmul_ev[(e, NIC - 1)])
            for q in range(4):
                slot = q % 2
                if q < 3:
                    if (e, q - 1) in dq_fr:
                        p.wait("gpsimd", dq_fr[(e, q - 1)])
                    elif q == 0 and e >= 1:
                        p.wait("gpsimd", dq_fr[(e - 1, 3)])
                    d_ld[(e, q + 1)] = p.dma("gpsimd", "wdl", lambda g, e=e, q=q: g.dma_start(
                        out=bfv(WD + ((q + 1) % 2) * 3072, [[512, NIC], [1, 512]]),
                        in_=AP(wdown_d, e * 128 * NIC * HD + (q + 1) * 512, [[NIC * HD, 128], [HD, NIC], [1, 512]])))
                p.wait("tensor", d_ld[(e, q)])
                for tt in range(NT):
                    u = q * NT + tt
                    if (e, u - 2) in xacc_evs:
                        p.wait("tensor", xacc_evs[(e, u - 2)])
                    elif e > 0 and u < 2:
                        p.wait("tensor", xacc_evs[(e - 1, 4 * NT - 2 + u)])

                    def dmm(t, tt=tt, q=q, slot=slot, pu=u % 2):
                        for ic in range(NIC):
                            ins = t.matmul(ps_y[pu], hts(ic, tt), wdv(slot, ic),
                                           start=(ic == 0), stop=(ic == NIC - 1))
                        return ins
                    ev_y = p.sig("tensor", "tm", dmm)
                    p.wait("vector", ev_y)

                    def xacc(v, tt=tt, q=q, e=e, pu=u % 2):
                        return v.scalar_tensor_tensor(
                            x_sb[:, tt, q * 512:(q + 1) * 512], ps_y[pu],
                            cmat[:, tt, e:e + 1],
                            x_sb[:, tt, q * 512:(q + 1) * 512],
                            op0=ALU.mult, op1=ALU.add)
                    xacc_evs[(e, u)] = last_x = p.sig("vector", "vt", xacc)
                dq_fr[(e, q)] = p.val("tm")
        p.wait_now("vector", "vt")
        p.play()

    emit_ln("a")
    emit_routing("a")
    emit_experts("a")

    emit_ln("b")

    WM, QS, VS = 0, 16384, 20480
    wmv = lambda ec, fc: bfv(WM + ec * 1024 + fc * 512, [[1, 512]])
    qsv = lambda par, fh: bfv(QS + par * HD + fh * 1024, [[1, 1024]])
    vsv = lambda par, fh: bfv(VS + par * NH * (DH + 1) + fh * 8 * (DH + 1), [[DH + 1, 8], [1, DH]])
    ps_m = [psA[:, 0:1024], psA[:, 1024:2048]]
    p.wait_now("vector", "st")
    p.wait_now("vector", "tm")
    p.sig("vector", "vt", lambda v: v.memset(bfv(VS, [[1, 2 * NH * (DH + 1)]]), 1.0))
    for mi, (w_dram, m, dst) in enumerate(((wq_d, "q", q_dr), (wk_d, "k", k_dr), (wv_d, "v", v_dr))):
        ev_evac, ev_store = {}, {}
        for fh in range(2):
            p.wait_now("gpsimd", "tm")
            p.wait_now("gpsimd", "st")
            p.wait_now("gpsimd", "vt")
            ev_w = p.dma("gpsimd", "gq", lambda g, w_dram=w_dram, fh=fh: g.dma_start(
                out=bfv(WM, [[1, NEC * 1024]]),
                in_=AP(w_dram, fh * 1024, [[NEC * HD, 128], [HD, NEC], [1, 1024]])))
            p.wait("tensor", ev_w)
            p.wait_now("tensor", "ld")
            p.wait_now("tensor", "vt")
            for tt in range(NT):
                g2 = tt
                if g2 >= 2:
                    p.wait("tensor", ev_evac[(fh, g2 - 2)])

                def qmm(t, tt=tt, fh=fh, ps=ps_m[tt % 2], m=m):
                    for ec in range(NEC):
                        for fc in range(2):
                            t.matmul(ps[:, fc * 512:(fc + 1) * 512], xlnT[:, ec, tt * 128:(tt + 1) * 128],
                                     wmv(ec, fc), start=(ec == 0), stop=False)
                    for fc in range(2):
                        ins = t.matmul(ps[:, fc * 512:(fc + 1) * 512], ones_r[:1, 0:128],
                                       bias_bf[m][:1, fh * 1024 + fc * 512:fh * 1024 + (fc + 1) * 512],
                                       start=False, stop=True)
                    return ins
                ev_mm = p.sig("tensor", "tm", qmm)
                p.wait("vector", ev_mm)
                if g2 >= 2:
                    p.wait("vector", ev_store[(fh, g2 - 2)])
                elif fh == 1:
                    p.wait("vector", ev_store[(0, NT - 2 + g2)])
                if m == "v":
                    ev_evac[(fh, g2)] = p.sig("vector", "vt", lambda v, tt=tt, fh=fh, ps=ps_m[tt % 2]: v.tensor_copy(
                        vsv(tt % 2, fh), ps.rearrange("a (h d) -> a h d", h=8)))
                elif m == "q":
                    ev_evac[(fh, g2)] = p.sig("vector", "vt", lambda v, tt=tt, fh=fh, ps=ps_m[tt % 2]: v.tensor_scalar(
                        qsv(tt % 2, fh), ps, C(SCALING), None, op0=ALU.mult))
                else:
                    ev_evac[(fh, g2)] = p.sig("vector", "vt", lambda v, tt=tt, fh=fh, ps=ps_m[tt % 2]: v.tensor_copy(
                        qsv(tt % 2, fh), ps))
                p.wait("sync", ev_evac[(fh, g2)])
                if m == "v":
                    ev_store[(fh, g2)] = p.dma("sync", "st", lambda s, tt=tt, fh=fh: s.dma_start(
                        out=AP(v_dr, tt * 128 * (DH + 1) + fh * 8 * TLOC * (DH + 1),
                               [[DH + 1, 128], [TLOC * (DH + 1), 8], [1, DH + 1]]),
                        in_=AP(arena_bf, VS + (tt % 2) * NH * (DH + 1) + fh * 8 * (DH + 1),
                               [[ABF, 128], [DH + 1, 8], [1, DH + 1]])))
                else:
                    ev_store[(fh, g2)] = p.dma("sync", "st", lambda s, tt=tt, fh=fh, dst=dst: s.dma_start(
                        out=dst[tt * 128:(tt + 1) * 128, fh * 1024:(fh + 1) * 1024], in_=qsv(tt % 2, fh)))
    p.play()

    p.wait_now("gpsimd", "st")
    ev_k = p.sig("gpsimd", "ccs", lambda g: g.collective_compute(
        "AllGather", ALU.bypass, replica_groups=[[0, 1, 2, 3], [4, 5, 6, 7]],
        ins=[k_dr.ap().opt()], outs=[kall_dr.ap().opt()]))
    ev_v = p.sig("gpsimd", "ccs", lambda g: g.collective_compute(
        "AllGather", ALU.bypass, replica_groups=[[0, 1, 2, 3], [4, 5, 6, 7]],
        ins=[v_dr.ap().opt()], outs=[vall_dr.ap().opt()]))

    KH, VH, PR, MAB = 0, 4096, 8224, 9248
    qT = xlnT
    khl = lambda off, n: bfv(KH + off, [[1, n]])
    vhv = lambda jt: bfv(VH + jt * (DH + 1), [[1, DH + 1]])
    prv = lambda k2, a, n: bfv(PR + k2 * 512 + a, [[1, n]])
    mabs = lambda w0: AP(arena_bf, MAB + 2 * w0, [[ABF, 128], [1, 1024]]).bitcast(F32)
    sttv = lambda k2: fv(1024 + k2 * 512, [[1, 512]])
    ps_s = [psA[:, 0:512], psA[:, 512:1024]]
    ps_o = [psB[:, 0:DH + 1], psB[:, 512:512 + DH + 1], psA[:, 1024:1024 + DH + 1], psA[:, 1536:1536 + DH + 1]]
    ps_bM = psC[:, 0:1]

    p.wait_now("sync", "st")
    p.wait_now("sync", "tm")
    p.wait_now("sync", "vt")
    for ec in range(NEC):
        p.dma("sync", "ld", lambda s, ec=ec: s.dma_start(out=qT[:, ec, :], in_=q_dr[:, ec * 128:(ec + 1) * 128], transpose=True))
    p.dma("sync", "ld", lambda s: s.dma_start(out=AP(arena_bf, MAB, [[ABF, 128], [1, 9984]]).bitcast(F32), in_=absm_d[:, :]))
    p.wait("sync", ev_k)
    p.wait("sync", ev_v)

    kv_consumed = {}
    stt_evs, exp_evs, pv_evs, evac_evs, evst = {}, {}, {}, {}, {}
    for h in range(NH):
        if h >= 1:
            p.wait("sync", kv_consumed[h - 1])
        for gch in range(8):
            ev_kl = p.dma("sync", "ld", lambda s, h=h, gch=gch: s.dma_start(
                out=khl(gch * 512, 512),
                in_=kall_dr[gch * 512:(gch + 1) * 512, h * 128:(h + 1) * 128], transpose=True))
        for cp in range(4):
            ev_vl = p.dma("sync", "ld", lambda s, h=h, cp=cp: s.dma_start(
                out=AP(arena_bf, VH + cp * 8 * (DH + 1), [[ABF, 128], [DH + 1, 8], [1, DH + 1]]),
                in_=AP(vall_dr, (cp * NH + h) * TLOC * (DH + 1),
                       [[DH + 1, 128], [128 * (DH + 1), 8], [1, DH + 1]])))
        p.wait("tensor", ev_kl)
        p.wait("tensor", ev_vl)
        p.wait_now("tensor", "ld")
        p.wait_now("tensor", "vt")

        red_evs = {}
        for jt in range(NJT):
            for ih in range(2):
                g1 = 2 * jt + ih
                if g1 >= 2:
                    p.wait("tensor", red_evs[g1 - 2])
                ev_mm = p.sig("tensor", "tm", lambda t, h=h, jt=jt, ih=ih, g1=g1: t.matmul(
                    ps_s[g1 % 2], khl(jt * 128, 128),
                    qT[:, h, ih * 512:(ih + 1) * 512], start=True, stop=True))
                p.wait("vector", ev_mm)
                red_evs[g1] = p.sig("vector", "vt", lambda v, g1=g1: v.reduce_max(
                    cmstrip[:, g1:g1 + 1], ps_s[g1 % 2], axis=AX))
        ev = p.sig("vector", "vt", lambda v: v.reduce_max(colM[:, :], cmstrip[:, :], axis=AX))
        p.wait("vector", ev)
        ev = p.sig("vector", "vt", lambda v: v.tensor_copy(colM_bf[:, :], colM[:, :]))
        p.wait("tensor", ev)
        ev = p.sig("tensor", "tm", lambda t: t.transpose(ps_tr[:1, :], colM_bf[:, :], id_sb[:, :]))
        p.wait("vector", ev)
        ev = p.sig("vector", "vt", lambda v: v.reduce_max(msr[:1, :], ps_tr[:1, :], axis=AX))
        p.wait("tensor", ev)
        ev = p.sig("tensor", "tm", lambda t: t.matmul(ps_bM, ones_r[:1, :], msr[:1, :], start=True, stop=True))
        p.wait("vector", ev)
        p.wait_now("vector", "sc")
        ev_negM = p.sig("vector", "vt", lambda v: v.tensor_scalar(negM[:, :], ps_bM, C(-1.0), None, op0=ALU.mult))

        for ic in range(2):
            gidx = h * 2 + ic
            for jt in range(NJT):
                g2 = gidx * NJT + jt
                k2 = jt % 2
                if jt >= 2:
                    p.wait("tensor", stt_evs[g2 - 2])
                if jt < 2:
                    p.wait_now("tensor", "vt")
                ev_mm = p.sig("tensor", "tm", lambda t, h=h, jt=jt, ic=ic, k2=k2: t.matmul(
                    ps_s[k2], khl(jt * 128, 128),
                    qT[:, h, ic * 512:(ic + 1) * 512], start=True, stop=True))
                p.wait("vector", ev_mm)
                if g2 >= 2:
                    p.wait("vector", exp_evs[g2 - 2])
                w0 = ic * 512 - jt * 128 + 3968
                stt_evs[g2] = p.sig("vector", "vt", lambda v, h=h, k2=k2, w0=w0: v.scalar_tensor_tensor(
                    sttv(k2), mabs(w0)[:, 0:512], negslope[:, h:h + 1], ps_s[k2],
                    op0=ALU.mult, op1=ALU.add))
                p.wait("scalar", stt_evs[g2])
                if jt == 0:
                    p.wait("scalar", ev_negM)
                if g2 >= 2:
                    p.wait("scalar", pv_evs[g2 - 2])
                exp_evs[g2] = p.sig("scalar", "sc", lambda s, k2=k2: s.activation(
                    out=prv(k2, 0, 512), in_=sttv(k2), func=ACTF.Exp, bias=negM[:, 0:1], scale=1.0))
                p.wait("tensor", exp_evs[g2])
                if jt == 0 and gidx >= 1:
                    for it in range(4):
                        p.wait("tensor", evac_evs[(gidx - 1, it)])

                def pv(t, h=h, jt=jt, k2=k2):
                    for it in range(4):
                        ins = t.matmul(ps_o[it], prv(k2, it * 128, 128),
                                       vhv(jt), start=(jt == 0), stop=(jt == NJT - 1))
                    return ins
                pv_evs[g2] = p.sig("tensor", "tm", pv)
            for it in range(4):
                gst = gidx * 4 + it
                p.wait("vector", pv_evs[gidx * NJT + NJT - 1])
                ev = p.sig("vector", "vt", lambda v, it=it: v.tensor_copy(scol[:, :], ps_o[it][:, DH:DH + 1]))
                p.wait("vector", ev)
                ev = p.sig("vector", "vt", lambda v: v.reciprocal(rscol[:, :], scol[:, :]))
                p.wait("vector", ev)
                if gst >= 2:
                    p.wait("vector", evst[gst - 2])
                ev2 = p.sig("vector", "vt", lambda v, it=it, gst=gst: v.tensor_tensor(
                    attn_scr[:, gst % 2, :], ps_o[it][:, 0:DH], rscol[:, 0:1].to_broadcast([128, DH]), op=ALU.mult))
                evac_evs[(gidx, it)] = ev2
                p.wait("sync", ev2)
                evst[gst] = p.dma("sync", "st", lambda s, h=h, ic=ic, it=it, gst=gst: s.dma_start(
                    out=attn_dr[ic * 512 + it * 128: ic * 512 + (it + 1) * 128, h * 128:(h + 1) * 128],
                    in_=attn_scr[:, gst % 2, :]))
        kv_consumed[h] = p.val("tm")
    p.play()

    attnT = xlnT
    wov = lambda ec, fc: bfv(ec * 1024 + fc * 512, [[1, 512]])
    ps_w = [psA[:, 0:1024], psA[:, 1024:2048]]
    p.wait_now("sync", "st")
    p.wait_now("sync", "tm")
    for ec in range(NEC):
        p.dma("sync", "ld", lambda s, ec=ec: s.dma_start(out=attnT[:, ec, :], in_=attn_dr[:, ec * 128:(ec + 1) * 128], transpose=True))
    for fh in range(2):
        p.wait_now("gpsimd", "tm")
        ev_w = p.dma("gpsimd", "gq", lambda g, fh=fh: g.dma_start(
            out=bfv(0, [[1, NEC * 1024]]), in_=AP(wo_d, fh * TLOC, [[NEC * HD, 128], [HD, NEC], [1, TLOC]])))
        p.wait("tensor", ev_w)
        p.wait_now("tensor", "ld")
        p.wait_now("tensor", "vt")
        xwo_ev = {}
        for tt in range(NT):
            if tt >= 2:
                p.wait("tensor", xwo_ev[tt - 2])

            def womm(t, tt=tt, fh=fh, ps=ps_w[tt % 2]):
                for ec in range(NEC):
                    for fc in range(2):
                        t.matmul(ps[:, fc * 512:(fc + 1) * 512], attnT[:, ec, tt * 128:(tt + 1) * 128],
                                 wov(ec, fc), start=(ec == 0), stop=False)
                for fc in range(2):
                    ins = t.matmul(ps[:, fc * 512:(fc + 1) * 512], ones_r[:1, 0:128],
                                   bias_bf["o"][:1, fh * TLOC + fc * 512:fh * TLOC + (fc + 1) * 512],
                                   start=False, stop=True)
                return ins
            ev_mm = p.sig("tensor", "tm", womm)
            p.wait("vector", ev_mm)

            def xadd(v, tt=tt, fh=fh, ps=ps_w[tt % 2]):
                for fc in range(2):
                    ins = v.tensor_add(x_sb[:, tt, fh * TLOC + fc * 512:fh * TLOC + (fc + 1) * 512],
                                       x_sb[:, tt, fh * TLOC + fc * 512:fh * TLOC + (fc + 1) * 512],
                                       ps[:, fc * 512:(fc + 1) * 512])
                return ins
            xwo_ev[tt] = p.sig("vector", "vt", xadd)
    p.wait_now("vector", "vt")
    p.play()

    emit_ln("c")
    emit_routing("c")
    emit_experts("c")

    p.wait_now("sync", "vt")
    for tt in range(NT):
        p.dma("sync", "st", lambda s, tt=tt: s.dma_start(out=out_d[tt * 128:(tt + 1) * 128, :], in_=x_sb[:, tt, :]))
    p.wait_now("sync", "st")
    p.play()

    ctx.close()
    return nc


def _prep_weights(inputs):
    def chunkT(w):
        wt = np.ascontiguousarray(w.T)
        return np.ascontiguousarray(wt.reshape(NEC, 128, HD).transpose(1, 0, 2).reshape(128, NEC * HD))

    d = {}
    d["wq"] = chunkT(inputs["L0_attn_Wq"])
    d["wk"] = chunkT(inputs["L0_attn_Wk"])
    d["wv"] = chunkT(inputs["L0_attn_Wv"])
    d["wo"] = chunkT(inputs["L0_attn_Wo"])
    d["bq"] = np.ascontiguousarray(inputs["L0_attn_bq"])
    d["bk"] = np.ascontiguousarray(inputs["L0_attn_bk"])
    d["bv"] = np.ascontiguousarray(inputs["L0_attn_bv"])
    d["bo"] = np.ascontiguousarray(inputs["L0_attn_bo"])
    d["slopes"] = np.ascontiguousarray(inputs["L0_attn_slopes"]).reshape(1, NH)
    d["gate_w"] = np.ascontiguousarray(inputs["L0_ffn_gate_w"].reshape(NEC, 128, E).transpose(1, 0, 2))
    d["wgate"] = np.ascontiguousarray(
        inputs["L0_ffn_w_gate"].reshape(E, NEC, 128, INTER).transpose(0, 2, 1, 3).reshape(E, 128, NEC * INTER))
    d["wup"] = np.ascontiguousarray(
        inputs["L0_ffn_w_up"].reshape(E, NEC, 128, INTER).transpose(0, 2, 1, 3).reshape(E, 128, NEC * INTER))
    d["wdown"] = np.ascontiguousarray(
        inputs["L0_ffn_w_down"].reshape(E, NIC, 128, HD).transpose(0, 2, 1, 3).reshape(E, 128, NIC * HD))
    d["tri"] = np.ascontiguousarray(np.tril(np.ones((128, 128), np.float32), -1).T).astype(ml_dtypes.bfloat16)
    d["ident"] = np.eye(128, dtype=np.float32).astype(ml_dtypes.bfloat16)
    return d


def kernel(L0_attn_x, L0_attn_Wq, L0_attn_bq, L0_attn_Wk, L0_attn_bk, L0_attn_Wv,
           L0_attn_bv, L0_attn_Wo, L0_attn_bo, L0_attn_slopes, L0_ffn_gate_w,
           L0_ffn_w_gate, L0_ffn_w_up, L0_ffn_w_down):
    inputs = dict(L0_attn_x=np.asarray(L0_attn_x), L0_attn_Wq=np.asarray(L0_attn_Wq),
                  L0_attn_bq=np.asarray(L0_attn_bq), L0_attn_Wk=np.asarray(L0_attn_Wk),
                  L0_attn_bk=np.asarray(L0_attn_bk), L0_attn_Wv=np.asarray(L0_attn_Wv),
                  L0_attn_bv=np.asarray(L0_attn_bv), L0_attn_Wo=np.asarray(L0_attn_Wo),
                  L0_attn_bo=np.asarray(L0_attn_bo), L0_attn_slopes=np.asarray(L0_attn_slopes),
                  L0_ffn_gate_w=np.asarray(L0_ffn_gate_w), L0_ffn_w_gate=np.asarray(L0_ffn_w_gate),
                  L0_ffn_w_up=np.asarray(L0_ffn_w_up), L0_ffn_w_down=np.asarray(L0_ffn_w_down))
    shared = _prep_weights(inputs)
    x = inputs["L0_attn_x"]

    nc = build_program()
    in_maps = []
    for c in range(NCORES):
        b, r = c // 4, c % 4
        m = dict(shared)
        m["x_in"] = np.ascontiguousarray(x[b, r * TLOC:(r + 1) * TLOC, :])
        qbase = r * TLOC
        w = np.arange(4992, dtype=np.float32)[None, :]
        q = np.arange(128, dtype=np.float32)[:, None]
        m["absm"] = np.ascontiguousarray(np.abs(w - 3968.0 + qbase - q))
        cm = np.zeros((8, 1), np.float32)
        cm[:c] = 1.0
        m["corem"] = cm
        in_maps.append(m)

    res = run_bass_kernel_spmd(nc, in_maps, core_ids=list(range(NCORES)))
    out = np.empty((B, T, HD), np.float32)
    for c in range(NCORES):
        b, r = c // 4, c % 4
        out[b, r * TLOC:(r + 1) * TLOC, :] = res.results[c]["out"]
    return out


# version 04
# speedup vs baseline: 1.0872x; 1.0097x over previous
import numpy as np
import ml_dtypes
import concourse.bass as bass
import concourse.mybir as mybir
from concourse.bass_types import AP
from concourse.bass_utils import run_bass_kernel_spmd
from contextlib import ExitStack

F32 = mybir.dt.float32
BF16 = mybir.dt.bfloat16
ALU = mybir.AluOpType
ACTF = mybir.ActivationFunctionType
AX = mybir.AxisListType.X

NCORES = 8
B, T, HD = 2, 4096, 2048
TLOC = 1024
NT = TLOC // 128
NEC = HD // 128
NH, DH = 16, 128
E, INTER = 32, 768
NIC = INTER // 128
CAP = 4096.0
NJT = T // 128
SCALING = float(DH) ** -0.5

CONSTS = [1.0, 0.0, -1.0, 0.5, CAP, 1.0 / HD, 1e-5, SCALING]
ENGS = ("sync", "gpsimd", "vector", "scalar", "tensor")

ABF = 37888
AF32 = 2048


class Prog:
    def __init__(self, nc, ctx):
        self.nc = nc
        self.cmds = {e: [] for e in ENGS}
        self.sems = {}
        self.ctx = ctx

    def sem(self, name):
        h = self.ctx.enter_context(self.nc.semaphore(name))
        self.sems[name] = [h, 0]

    def emit(self, eng, fn):
        self.cmds[eng].append(fn)

    def sig(self, eng, sem, fn, k=1):
        rec = self.sems[sem]
        rec[1] += k
        val, h = rec[1], rec[0]

        def cmd(e, fn=fn, h=h, k=k):
            fn(e).then_inc(h, k)
        self.cmds[eng].append(cmd)
        return (sem, val)

    def dma(self, eng, sem, fn):
        return self.sig(eng, sem, fn, 16)

    def val(self, sem):
        return (sem, self.sems[sem][1])

    def wait(self, eng, ev):
        sem, v = ev
        if v <= 0:
            return
        h = self.sems[sem][0]
        self.cmds[eng].append(lambda e, h=h, v=v: e.wait_ge(h, v))

    def wait_now(self, eng, sem):
        self.wait(eng, self.val(sem))

    def play(self):
        with self.nc.Block() as blk:
            for name in ENGS:
                cmds = self.cmds[name]

                def runner(eng, cmds=cmds):
                    for c in cmds:
                        c(eng)
                getattr(blk, name)(runner)
        self.cmds = {e: [] for e in ENGS}


def build_program():
    nc = bass.Bass("TRN2", target_bir_lowering=False)

    x_in = nc.dram_tensor("x_in", [TLOC, HD], F32, kind="ExternalInput")
    out_d = nc.dram_tensor("out", [TLOC, HD], F32, kind="ExternalOutput")
    wq_d = nc.dram_tensor("wq", [128, NEC * HD], F32, kind="ExternalInput")
    wk_d = nc.dram_tensor("wk", [128, NEC * HD], F32, kind="ExternalInput")
    wv_d = nc.dram_tensor("wv", [128, NEC * HD], F32, kind="ExternalInput")
    wo_d = nc.dram_tensor("wo", [128, NEC * HD], F32, kind="ExternalInput")
    bq_d = nc.dram_tensor("bq", [HD], F32, kind="ExternalInput")
    bk_d = nc.dram_tensor("bk", [HD], F32, kind="ExternalInput")
    bv_d = nc.dram_tensor("bv", [HD], F32, kind="ExternalInput")
    bo_d = nc.dram_tensor("bo", [HD], F32, kind="ExternalInput")
    slopes_d = nc.dram_tensor("slopes", [1, NH], F32, kind="ExternalInput")
    gate_d = nc.dram_tensor("gate_w", [128, NEC, E], F32, kind="ExternalInput")
    wgate_d = nc.dram_tensor("wgate", [E, 128, NEC * INTER], F32, kind="ExternalInput")
    wup_d = nc.dram_tensor("wup", [E, 128, NEC * INTER], F32, kind="ExternalInput")
    wdown_d = nc.dram_tensor("wdown", [E, 128, NIC * HD], F32, kind="ExternalInput")
    tri_d = nc.dram_tensor("tri", [128, 128], BF16, kind="ExternalInput")
    ident_d = nc.dram_tensor("ident", [128, 128], BF16, kind="ExternalInput")
    absm_d = nc.dram_tensor("absm", [128, 4992], F32, kind="ExternalInput")
    corem_d = nc.dram_tensor("corem", [8, 1], F32, kind="ExternalInput")
    xln_dr = nc.dram_tensor("xln_d", [TLOC, HD], BF16)
    q_dr = nc.dram_tensor("q_d", [TLOC, HD], BF16)
    k_dr = nc.dram_tensor("k_d", [TLOC, HD], BF16)
    v_dr = nc.dram_tensor("v_d", [NH, TLOC, DH + 1], BF16)
    attn_dr = nc.dram_tensor("attn_d", [TLOC, HD], BF16)
    kall_dr = nc.dram_tensor("k_all", [T, HD], BF16)
    vall_dr = nc.dram_tensor("v_all", [4 * NH, TLOC, DH + 1], BF16)
    cntin_dr = nc.dram_tensor("cnt_in", [1, E], F32)
    cntout_dr = nc.dram_tensor("cnt_out", [8, E], F32)

    ctx = ExitStack()
    sbt = lambda n, s, d=F32: ctx.enter_context(nc.sbuf_tensor(n, s, d))
    p = Prog(nc, ctx)
    for s in ("ld", "gq", "vt", "tm", "sc", "st", "ccs", "wgl", "wul", "wdl"):
        p.sem(s)

    cst = sbt("cst", [128, len(CONSTS)])
    CIDX = {v: i for i, v in enumerate(CONSTS)}
    C = lambda v: cst[:, CIDX[v]:CIDX[v] + 1]

    x_sb = sbt("x_sb", [128, NT, HD])
    xlnT = sbt("xlnT", [128, NEC, TLOC], BF16)
    tri_sb = sbt("tri_sb", [128, 128], BF16)
    id_sb = sbt("id_sb", [128, 128], BF16)
    corem_sb = sbt("corem_sb", [8, 1], BF16)
    ones_r = sbt("ones_r", [1, 128], BF16)
    ones_c = sbt("ones_c", [128, 1], BF16)
    gatew_sb = sbt("gatew_sb", [128, NEC, E], BF16)
    slopes_sb = sbt("slopes_sb", [1, NH])
    slopes_bf = sbt("slopes_bf", [1, NH], BF16)
    negslope = sbt("negslope", [128, NH])
    bias_bf = {m: sbt(f"biasbf_{m}", [1, HD], BF16) for m in "qkvo"}
    cmat = sbt("cmat", [128, NT, E])
    lgt = sbt("lgt", [128, NT, E])
    mxf = sbt("mxf", [128, NT])
    nmxf = sbt("nmxf", [128, NT])
    exf = sbt("exf", [128, NT, E])
    smf = sbt("smf", [128, NT])
    rsmf = sbt("rsmf", [128, NT])
    prf = sbt("prf", [128, E])
    w8f = sbt("w8f", [128, 8])
    denf = sbt("denf", [128, 1])
    rdenf = sbt("rdenf", [128, 1])
    smask = sbt("smask", [128, NT, E])
    smask_bf = sbt("smask_bf", [128, NT, E], BF16)
    cum = sbt("cum", [128, NT, E])
    off_row = sbt("off_row", [1, E], BF16)
    cnt_f = sbt("cnt_f", [1, E])
    agcnt = sbt("agcnt", [8, E], BF16)
    goff_row = sbt("goff_row", [1, E], BF16)
    capv = sbt("capv", [128, E])
    keepm = sbt("keepm", [128, E])
    mcol = sbt("mcol", [128, 1])
    ssq = sbt("ssq", [128, 1])
    stdc = sbt("stdc", [128, 1])
    rstd = sbt("rstd", [128, 1])
    cmstrip = sbt("cmstrip", [128, 2 * NJT])
    colM = sbt("colM", [128, 1])
    colM_bf = sbt("colM_bf", [128, 1], BF16)
    msr = sbt("msr", [1, 1], BF16)
    negM = sbt("negM", [128, 1])
    scol = sbt("scol", [128, 1])
    rscol = sbt("rscol", [128, 1])
    attn_scr = sbt("attn_scr", [128, 2, DH], BF16)

    arena_bf = sbt("arena_bf", [128, ABF], BF16)
    arena_f32 = sbt("arena_f32", [128, AF32], F32)

    def bfv(off, dims):
        return AP(arena_bf, off, [[ABF, 128]] + dims)

    def fv(off, dims):
        return AP(arena_f32, off, [[AF32, 128]] + dims)

    psA = ctx.enter_context(nc.psum_tensor("psA", [128, 2048], F32))
    psB = ctx.enter_context(nc.psum_tensor("psB", [128, 1024], F32))
    psC = ctx.enter_context(nc.psum_tensor("psC", [128, 512], F32))
    ps_tr = ctx.enter_context(nc.psum_tensor("ps_tr", [1, 128], BF16))

    for tt in range(NT):
        p.dma("sync", "ld", lambda s, tt=tt: s.dma_start(out=x_sb[:, tt, :], in_=x_in[tt * 128:(tt + 1) * 128, :]))
    p.dma("sync", "ld", lambda s: s.dma_start(out=tri_sb[:, :], in_=tri_d[:, :]))
    p.dma("sync", "ld", lambda s: s.dma_start(out=id_sb[:, :], in_=ident_d[:, :]))
    p.dma("sync", "ld", lambda s: s.dma_start(out=slopes_sb[:, :], in_=slopes_d[:, :]))
    for m, bsrc in (("q", bq_d), ("k", bk_d), ("v", bv_d), ("o", bo_d)):
        p.dma("gpsimd", "gq", lambda g, m=m, bsrc=bsrc: g.dma_start(out=bias_bf[m][:1, :], in_=bsrc[None, :]))
    p.dma("gpsimd", "gq", lambda g: g.dma_start(out=gatew_sb[:, :, :], in_=gate_d[:, :, :]))
    p.dma("gpsimd", "gq", lambda g: g.dma_start(out=corem_sb[:, :], in_=corem_d[:, :]))

    for i, val in enumerate(CONSTS):
        p.emit("vector", lambda v, i=i, val=val: v.memset(cst[:, i:i + 1], val))
    p.emit("vector", lambda v: v.memset(ones_r[:, :], 1.0))
    p.emit("vector", lambda v: v.memset(ones_c[:, :], 1.0))
    p.wait_now("vector", "ld")
    p.wait_now("vector", "gq")
    p.sig("vector", "vt", lambda v: v.tensor_copy(slopes_bf[:1, :], slopes_sb[:1, :]))
    p.wait_now("tensor", "vt")
    p.sig("tensor", "tm", lambda t: t.matmul(psC[:, 0:NH], ones_r[:1, :], slopes_bf[:1, :], start=True, stop=True))
    p.wait_now("vector", "tm")
    p.sig("vector", "vt", lambda v: v.tensor_scalar(negslope[:, :], psC[:, 0:NH], C(-1.0), None, op0=ALU.mult))
    p.play()

    def emit_ln(tag):
        xc_t = fv(0, [[1, HD]])
        sqv = bfv(0, [[1, HD]])
        pp = lambda par: bfv(2048 + par * HD, [[1, HD]])
        p.wait_now("vector", "st")
        p.wait_now("vector", "tm")
        stev = {}
        for tt in range(NT):
            ev = p.sig("vector", "vt", lambda v, tt=tt: v.reduce_sum(mcol[:, :], x_sb[:, tt, :], axis=AX))
            p.wait("vector", ev)
            ev = p.sig("vector", "vt", lambda v: v.tensor_scalar(mcol[:, :], mcol[:, :], C(1.0 / HD), None, op0=ALU.mult))
            p.wait("vector", ev)
            p.emit("vector", lambda v, tt=tt: v.tensor_tensor(xc_t, x_sb[:, tt, :], mcol[:, 0:1].to_broadcast([128, HD]), op=ALU.subtract))
            ev_sq = p.sig("vector", "vt", lambda v: v.scalar_tensor_tensor(sqv, xc_t, C(1.0), xc_t, op0=ALU.mult, op1=ALU.mult, accum_out=ssq[:, :]))
            p.wait("scalar", ev_sq)
            ev_std = p.sig("scalar", "sc", lambda s: s.activation(out=stdc[:, :], in_=ssq[:, :], func=ACTF.Sqrt, bias=C(1e-5), scale=1.0 / HD))
            p.wait("vector", ev_std)
            ev = p.sig("vector", "vt", lambda v: v.reciprocal(rstd[:, :], stdc[:, :]))
            p.wait("vector", ev)
            if tt >= 2:
                p.wait("vector", stev[tt - 2])
            ev_x = p.sig("vector", "vt", lambda v, tt=tt: v.tensor_tensor(pp(tt % 2), xc_t, rstd[:, 0:1].to_broadcast([128, HD]), op=ALU.mult))
            p.wait("sync", ev_x)
            stev[tt] = p.dma("sync", "st", lambda s, tt=tt: s.dma_start(out=xln_dr[tt * 128:(tt + 1) * 128, :], in_=pp(tt % 2)))
        p.wait_now("sync", "st")
        p.wait_now("sync", "tm")
        for ec in range(NEC):
            p.dma("sync", "ld", lambda s, ec=ec: s.dma_start(out=xlnT[:, ec, :], in_=xln_dr[:, ec * 128:(ec + 1) * 128], transpose=True))
        p.play()

    def emit_routing(tag):
        ps_lg = [psA[:, 0:E], psA[:, 512:512 + E]]
        ps_cum = psA[:, 1024:1024 + E]
        ps_off = psA[:1, 1536:1536 + E]
        ps_goff = psB[:1, 0:E]
        ps_gbc = psB[:, 512:512 + E]

        p.wait_now("tensor", "ld")
        p.wait_now("tensor", "gq")
        evac_evs = []
        for tt in range(NT):
            if tt >= 2:
                p.wait("tensor", evac_evs[tt - 2])

            def gmm(t, tt=tt, pl=ps_lg[tt % 2]):
                for ec in range(NEC):
                    ins = t.matmul(pl, xlnT[:, ec, tt * 128:(tt + 1) * 128], gatew_sb[:, ec, :],
                                   start=(ec == 0), stop=(ec == NEC - 1))
                return ins
            ev_mm = p.sig("tensor", "tm", gmm)
            p.wait("vector", ev_mm)
            evac_evs.append(p.sig("vector", "vt", lambda v, tt=tt, pl=ps_lg[tt % 2]: v.tensor_copy(lgt[:, tt, :], pl)))

        for tt in range(NT):
            ev_red = p.sig("vector", "vt", lambda v, tt=tt: v.reduce_max(mxf[:, tt:tt + 1], lgt[:, tt, :], axis=AX))
        p.wait("vector", ev_red)
        ev_nm = p.sig("vector", "vt", lambda v: v.tensor_scalar(nmxf[:, :], mxf[:, :], C(-1.0), None, op0=ALU.mult))
        p.wait("scalar", ev_nm)
        for tt in range(NT):
            ev_ex = p.sig("scalar", "sc", lambda s, tt=tt: s.activation(out=exf[:, tt, :], in_=lgt[:, tt, :], func=ACTF.Exp, bias=nmxf[:, tt:tt + 1], scale=1.0, accum_out=smf[:, tt:tt + 1]))
        p.wait("vector", ev_ex)
        ev = p.sig("vector", "vt", lambda v: v.reciprocal(rsmf[:, :], smf[:, :]))
        p.wait("vector", ev)
        for tt in range(NT):
            ev = p.sig("vector", "vt", lambda v, tt=tt: v.tensor_tensor(prf[:, :], exf[:, tt, :], rsmf[:, tt:tt + 1].to_broadcast([128, E]), op=ALU.mult))
            p.wait("vector", ev)
            ev = p.sig("vector", "vt", lambda v: v.max(out=w8f[:, :], in_=prf[:, :]))
            p.wait("vector", ev)
            p.emit("vector", lambda v, tt=tt: v.tensor_scalar(smask[:, tt, :], prf[:, :], w8f[:, 7:8], None, op0=ALU.is_ge))
            ev = p.sig("vector", "vt", lambda v: v.reduce_sum(denf[:, :], w8f[:, :], axis=AX))
            p.wait("vector", ev)
            ev = p.sig("vector", "vt", lambda v: v.reciprocal(rdenf[:, :], denf[:, :]))
            p.wait("vector", ev)
            ev = p.sig("vector", "vt", lambda v: v.tensor_scalar(rdenf[:, :], rdenf[:, :], C(0.5), None, op0=ALU.mult))
            p.wait("vector", ev)
            p.emit("vector", lambda v, tt=tt: v.tensor_tensor(cmat[:, tt, :], prf[:, :], rdenf[:, 0:1].to_broadcast([128, E]), op=ALU.mult))
            p.emit("vector", lambda v, tt=tt: v.tensor_mul(cmat[:, tt, :], cmat[:, tt, :], smask[:, tt, :]))
        p.sig("vector", "vt", lambda v: v.memset(off_row[:1, :], 0.0))
        ev_sm = p.sig("vector", "vt", lambda v: v.tensor_copy(smask_bf[:, :, :], smask[:, :, :]))

        p.wait("tensor", ev_sm)
        off_evs = [p.val("vt")]
        for tt in range(NT):
            if tt > 0:
                p.wait("tensor", off_evs[tt])

            def cmm(t, tt=tt):
                t.matmul(ps_cum, tri_sb[:, :], smask_bf[:, tt, :], start=True, stop=False)
                t.matmul(ps_cum, ones_r[:1, :], off_row[:1, :], start=False, stop=True)
                return t.matmul(ps_off, ones_c[:, :1], smask_bf[:, tt, :], start=(tt == 0), stop=(tt == NT - 1))
            ev_mm = p.sig("tensor", "tm", cmm)
            p.wait("vector", ev_mm)
            p.sig("vector", "vt", lambda v, tt=tt: v.tensor_copy(cum[:, tt, :], ps_cum))
            off_evs.append(p.sig("vector", "vt", lambda v: v.tensor_copy(off_row[:1, :], ps_off)))
        ev_cnt = p.sig("vector", "vt", lambda v: v.tensor_copy(cnt_f[:1, :], ps_off))

        p.wait("gpsimd", ev_cnt)
        ev = p.dma("gpsimd", "st", lambda g: g.dma_start(out=cntin_dr[:, :], in_=cnt_f[:1, :]))
        p.wait("gpsimd", ev)
        ev = p.sig("gpsimd", "ccs", lambda g: g.collective_compute(
            "AllGather", ALU.bypass, replica_groups=[[0, 1, 2, 3, 4, 5, 6, 7]],
            ins=[cntin_dr.ap().opt()], outs=[cntout_dr.ap().opt()]))
        p.wait("gpsimd", ev)
        ev_ag = p.dma("gpsimd", "gq", lambda g: g.dma_start(out=agcnt[:, :], in_=cntout_dr[:, :]))
        p.wait("tensor", ev_ag)
        ev = p.sig("tensor", "tm", lambda t: t.matmul(ps_goff, corem_sb[:, :1], agcnt[:, :], start=True, stop=True))
        p.wait("vector", ev)
        ev = p.sig("vector", "vt", lambda v: v.tensor_copy(goff_row[:1, :], ps_goff))
        p.wait("tensor", ev)
        ev = p.sig("tensor", "tm", lambda t: t.matmul(ps_gbc, ones_r[:1, :], goff_row[:1, :], start=True, stop=True))
        p.wait("vector", ev)
        for tt in range(NT):
            ev = p.sig("vector", "vt", lambda v, tt=tt: v.tensor_add(capv[:, :], cum[:, tt, :], ps_gbc))
            p.wait("vector", ev)
            ev = p.sig("vector", "vt", lambda v: v.tensor_scalar(keepm[:, :], capv[:, :], C(CAP), None, op0=ALU.is_lt))
            p.wait("vector", ev)
            ev = p.sig("vector", "vt", lambda v, tt=tt: v.tensor_mul(cmat[:, tt, :], cmat[:, tt, :], keepm[:, :]))
            p.wait("vector", ev)
        p.play()

    def emit_experts(tag):
        WG, WU, WD, GI, HS = 0, 12288, 24576, 30720, 31744
        wgv = lambda ec, ic: bfv(WG + ec * INTER + ic * 128, [[1, 128]])
        wuv = lambda ec, ic: bfv(WU + ec * INTER + ic * 128, [[1, 128]])
        wdv = lambda slot, ic: bfv(WD + slot * 3072 + ic * 512, [[1, 512]])
        giv = bfv(GI, [[1, TLOC]])
        hv = lambda ic: bfv(HS + ic * TLOC, [[1, TLOC]])
        hts = lambda ic, tt: bfv(HS + ic * TLOC + tt * 128, [[1, 128]])
        ps_g = [psA[:, 0:1024], psA[:, 1024:2048]]
        ps_y = [psB[:, 0:512], psB[:, 512:1024]]

        g_fr, u_fr = {}, {}
        g_ld, u_ld, d_ld = {}, {}, {}
        silu_ev, hmul_ev = {}, {}
        dq_fr = {}
        p.wait_now("tensor", "vt")
        p.wait_now("gpsimd", "st")
        p.wait_now("gpsimd", "tm")
        p.wait_now("gpsimd", "vt")
        last_x = None
        xacc_evs = {}
        for e in range(E):
            if e >= 1:
                p.wait("gpsimd", g_fr[e - 1])
            g_ld[e] = p.dma("gpsimd", "wgl", lambda g, e=e: g.dma_start(
                out=bfv(WG, [[1, NEC * INTER]]), in_=wgate_d[e, :, :]))
            if e >= 1:
                p.wait("gpsimd", u_fr[e - 1])
            u_ld[e] = p.dma("gpsimd", "wul", lambda g, e=e: g.dma_start(
                out=bfv(WU, [[1, NEC * INTER]]), in_=wup_d[e, :, :]))
            if e >= 1:
                p.wait("gpsimd", dq_fr[(e - 1, 2)])
            d_ld[(e, 0)] = p.dma("gpsimd", "wdl", lambda g, e=e: g.dma_start(
                out=bfv(WD, [[512, NIC], [1, 512]]),
                in_=AP(wdown_d, e * 128 * NIC * HD, [[NIC * HD, 128], [HD, NIC], [1, 512]])))

            p.wait("tensor", g_ld[e])
            p.wait("tensor", u_ld[e])
            for ic in range(NIC):
                if (e, ic - 2) in hmul_ev:
                    p.wait("tensor", hmul_ev[(e, ic - 2)])
                elif e > 0 and ic < 2:
                    p.wait("tensor", hmul_ev[(e - 1, NIC - 2 + ic)])

                def gmm(t, ic=ic, ps=ps_g[ic % 2]):
                    for ec in range(NEC):
                        w = wgv(ec, ic)
                        t.matmul(ps[:, 0:512], w, xlnT[:, ec, 0:512], start=(ec == 0), stop=(ec == NEC - 1))
                        ins = t.matmul(ps[:, 512:1024], w, xlnT[:, ec, 512:1024], start=(ec == 0), stop=(ec == NEC - 1))
                    return ins
                ev_g = p.sig("tensor", "tm", gmm)
                p.wait("scalar", ev_g)
                if (e, ic - 1) in hmul_ev:
                    p.wait("scalar", hmul_ev[(e, ic - 1)])
                elif e > 0 and ic == 0:
                    p.wait("scalar", hmul_ev[(e - 1, NIC - 1)])
                silu_ev[(e, ic)] = p.sig("scalar", "sc", lambda s, ps=ps_g[ic % 2]: s.activation(out=giv, in_=ps[:, :], func=ACTF.Silu))
                p.wait("tensor", silu_ev[(e, ic)])

                def umm(t, ic=ic, ps=ps_g[ic % 2]):
                    for ec in range(NEC):
                        w = wuv(ec, ic)
                        t.matmul(ps[:, 0:512], w, xlnT[:, ec, 0:512], start=(ec == 0), stop=(ec == NEC - 1))
                        ins = t.matmul(ps[:, 512:1024], w, xlnT[:, ec, 512:1024], start=(ec == 0), stop=(ec == NEC - 1))
                    return ins
                ev_u = p.sig("tensor", "tm", umm)
                p.wait("vector", ev_u)
                p.wait("vector", silu_ev[(e, ic)])
                hmul_ev[(e, ic)] = p.sig("vector", "vt", lambda v, ic=ic, ps=ps_g[ic % 2]: v.tensor_tensor(hv(ic), ps[:, :], giv, op=ALU.mult))
            g_fr[e] = p.val("tm")
            u_fr[e] = p.val("tm")

            p.wait("tensor", hmul_ev[(e, NIC - 1)])
            for q in range(4):
                slot = q % 2
                if q < 3:
                    if (e, q - 1) in dq_fr:
                        p.wait("gpsimd", dq_fr[(e, q - 1)])
                    elif q == 0 and e >= 1:
                        p.wait("gpsimd", dq_fr[(e - 1, 3)])
                    d_ld[(e, q + 1)] = p.dma("gpsimd", "wdl", lambda g, e=e, q=q: g.dma_start(
                        out=bfv(WD + ((q + 1) % 2) * 3072, [[512, NIC], [1, 512]]),
                        in_=AP(wdown_d, e * 128 * NIC * HD + (q + 1) * 512, [[NIC * HD, 128], [HD, NIC], [1, 512]])))
                p.wait("tensor", d_ld[(e, q)])
                for tt in range(NT):
                    u = q * NT + tt
                    if (e, u - 2) in xacc_evs:
                        p.wait("tensor", xacc_evs[(e, u - 2)])
                    elif e > 0 and u < 2:
                        p.wait("tensor", xacc_evs[(e - 1, 4 * NT - 2 + u)])

                    def dmm(t, tt=tt, q=q, slot=slot, pu=u % 2):
                        for ic in range(NIC):
                            ins = t.matmul(ps_y[pu], hts(ic, tt), wdv(slot, ic),
                                           start=(ic == 0), stop=(ic == NIC - 1))
                        return ins
                    ev_y = p.sig("tensor", "tm", dmm)
                    p.wait("vector", ev_y)

                    def xacc(v, tt=tt, q=q, e=e, pu=u % 2):
                        return v.scalar_tensor_tensor(
                            x_sb[:, tt, q * 512:(q + 1) * 512], ps_y[pu],
                            cmat[:, tt, e:e + 1],
                            x_sb[:, tt, q * 512:(q + 1) * 512],
                            op0=ALU.mult, op1=ALU.add)
                    xacc_evs[(e, u)] = last_x = p.sig("vector", "vt", xacc)
                dq_fr[(e, q)] = p.val("tm")
        p.wait_now("vector", "vt")
        p.play()

    emit_ln("a")
    emit_routing("a")
    emit_experts("a")

    emit_ln("b")

    WM, QS, VS = 0, 16384, 20480
    wmv = lambda ec, fc: bfv(WM + ec * 1024 + fc * 512, [[1, 512]])
    qsv = lambda par, fh: bfv(QS + par * HD + fh * 1024, [[1, 1024]])
    vsv = lambda par, fh: bfv(VS + par * NH * (DH + 1) + fh * 8 * (DH + 1), [[DH + 1, 8], [1, DH]])
    ps_m = [psA[:, 0:1024], psA[:, 1024:2048]]
    p.wait_now("vector", "st")
    p.wait_now("vector", "tm")
    p.sig("vector", "vt", lambda v: v.memset(bfv(VS, [[1, 2 * NH * (DH + 1)]]), 1.0))
    for mi, (w_dram, m, dst) in enumerate(((wq_d, "q", q_dr), (wk_d, "k", k_dr), (wv_d, "v", v_dr))):
        ev_evac, ev_store = {}, {}
        for fh in range(2):
            p.wait_now("gpsimd", "tm")
            p.wait_now("gpsimd", "st")
            p.wait_now("gpsimd", "vt")
            ev_w = p.dma("gpsimd", "gq", lambda g, w_dram=w_dram, fh=fh: g.dma_start(
                out=bfv(WM, [[1, NEC * 1024]]),
                in_=AP(w_dram, fh * 1024, [[NEC * HD, 128], [HD, NEC], [1, 1024]])))
            p.wait("tensor", ev_w)
            p.wait_now("tensor", "ld")
            p.wait_now("tensor", "vt")
            for tt in range(NT):
                g2 = tt
                if g2 >= 2:
                    p.wait("tensor", ev_evac[(fh, g2 - 2)])

                def qmm(t, tt=tt, fh=fh, ps=ps_m[tt % 2], m=m):
                    for ec in range(NEC):
                        for fc in range(2):
                            t.matmul(ps[:, fc * 512:(fc + 1) * 512], xlnT[:, ec, tt * 128:(tt + 1) * 128],
                                     wmv(ec, fc), start=(ec == 0), stop=False)
                    for fc in range(2):
                        ins = t.matmul(ps[:, fc * 512:(fc + 1) * 512], ones_r[:1, 0:128],
                                       bias_bf[m][:1, fh * 1024 + fc * 512:fh * 1024 + (fc + 1) * 512],
                                       start=False, stop=True)
                    return ins
                ev_mm = p.sig("tensor", "tm", qmm)
                p.wait("vector", ev_mm)
                if g2 >= 2:
                    p.wait("vector", ev_store[(fh, g2 - 2)])
                elif fh == 1:
                    p.wait("vector", ev_store[(0, NT - 2 + g2)])
                if m == "v":
                    ev_evac[(fh, g2)] = p.sig("vector", "vt", lambda v, tt=tt, fh=fh, ps=ps_m[tt % 2]: v.tensor_copy(
                        vsv(tt % 2, fh), ps.rearrange("a (h d) -> a h d", h=8)))
                elif m == "q":
                    ev_evac[(fh, g2)] = p.sig("vector", "vt", lambda v, tt=tt, fh=fh, ps=ps_m[tt % 2]: v.tensor_scalar(
                        qsv(tt % 2, fh), ps, C(SCALING), None, op0=ALU.mult))
                else:
                    ev_evac[(fh, g2)] = p.sig("vector", "vt", lambda v, tt=tt, fh=fh, ps=ps_m[tt % 2]: v.tensor_copy(
                        qsv(tt % 2, fh), ps))
                p.wait("sync", ev_evac[(fh, g2)])
                if m == "v":
                    ev_store[(fh, g2)] = p.dma("sync", "st", lambda s, tt=tt, fh=fh: s.dma_start(
                        out=AP(v_dr, tt * 128 * (DH + 1) + fh * 8 * TLOC * (DH + 1),
                               [[DH + 1, 128], [TLOC * (DH + 1), 8], [1, DH + 1]]),
                        in_=AP(arena_bf, VS + (tt % 2) * NH * (DH + 1) + fh * 8 * (DH + 1),
                               [[ABF, 128], [DH + 1, 8], [1, DH + 1]])))
                else:
                    ev_store[(fh, g2)] = p.dma("sync", "st", lambda s, tt=tt, fh=fh, dst=dst: s.dma_start(
                        out=dst[tt * 128:(tt + 1) * 128, fh * 1024:(fh + 1) * 1024], in_=qsv(tt % 2, fh)))
    p.play()

    p.wait_now("gpsimd", "st")
    ev_k = p.sig("gpsimd", "ccs", lambda g: g.collective_compute(
        "AllGather", ALU.bypass, replica_groups=[[0, 1, 2, 3], [4, 5, 6, 7]],
        ins=[k_dr.ap().opt()], outs=[kall_dr.ap().opt()]))
    ev_v = p.sig("gpsimd", "ccs", lambda g: g.collective_compute(
        "AllGather", ALU.bypass, replica_groups=[[0, 1, 2, 3], [4, 5, 6, 7]],
        ins=[v_dr.ap().opt()], outs=[vall_dr.ap().opt()]))

    KH, VH, PR, MAB = 0, 4096, 8224, 9248
    qT = xlnT
    khl = lambda off, n: bfv(KH + off, [[1, n]])
    vhv = lambda jt: bfv(VH + jt * (DH + 1), [[1, DH + 1]])
    prv = lambda k2, a, n: bfv(PR + k2 * 512 + a, [[1, n]])
    mabs = lambda w0: AP(arena_bf, MAB + 2 * w0, [[ABF, 128], [1, 1024]]).bitcast(F32)
    sttv = lambda k2: fv(1024 + k2 * 512, [[1, 512]])
    ps_s = [psA[:, 0:512], psA[:, 512:1024]]
    ps_o = [psB[:, 0:DH + 1], psB[:, 512:512 + DH + 1], psA[:, 1024:1024 + DH + 1], psA[:, 1536:1536 + DH + 1]]
    ps_bM = psC[:, 0:1]

    p.wait_now("sync", "st")
    p.wait_now("sync", "tm")
    p.wait_now("sync", "vt")
    for ec in range(NEC):
        p.dma("sync", "ld", lambda s, ec=ec: s.dma_start(out=qT[:, ec, :], in_=q_dr[:, ec * 128:(ec + 1) * 128], transpose=True))
    p.dma("sync", "ld", lambda s: s.dma_start(out=AP(arena_bf, MAB, [[ABF, 128], [1, 9984]]).bitcast(F32), in_=absm_d[:, :]))
    p.wait("sync", ev_k)
    p.wait("sync", ev_v)

    kv_consumed = {}
    stt_evs, exp_evs, pv_evs, evac_evs, evst = {}, {}, {}, {}, {}
    for h in range(NH):
        if h >= 1:
            p.wait("sync", kv_consumed[h - 1])
        for gch in range(8):
            ev_kl = p.dma("sync", "ld", lambda s, h=h, gch=gch: s.dma_start(
                out=khl(gch * 512, 512),
                in_=kall_dr[gch * 512:(gch + 1) * 512, h * 128:(h + 1) * 128], transpose=True))
        for cp in range(4):
            ev_vl = p.dma("sync", "ld", lambda s, h=h, cp=cp: s.dma_start(
                out=AP(arena_bf, VH + cp * 8 * (DH + 1), [[ABF, 128], [DH + 1, 8], [1, DH + 1]]),
                in_=AP(vall_dr, (cp * NH + h) * TLOC * (DH + 1),
                       [[DH + 1, 128], [128 * (DH + 1), 8], [1, DH + 1]])))
        p.wait("tensor", ev_kl)
        p.wait("tensor", ev_vl)
        p.wait_now("tensor", "ld")
        p.wait_now("tensor", "vt")

        red_evs = {}
        s1ps = [psA[:, 0:512], psA[:, 512:1024], psA[:, 1024:1536], psA[:, 1536:2048]]
        for jt in range(NJT):
            sl = (jt % 2) * 2
            if jt >= 2:
                p.wait("tensor", red_evs[jt - 2])
            def s1mm(t, h=h, jt=jt, sl=sl):
                t.matmul(s1ps[sl], khl(jt * 128, 128), qT[:, h, 0:512], start=True, stop=True)
                return t.matmul(s1ps[sl + 1], khl(jt * 128, 128), qT[:, h, 512:1024], start=True, stop=True)
            ev_mm = p.sig("tensor", "tm", s1mm)
            p.wait("vector", ev_mm)
            red_evs[jt] = p.sig("vector", "vt", lambda v, jt=jt, sl=sl: v.reduce_max(
                cmstrip[:, jt:jt + 1], psA[:, sl * 512:(sl + 2) * 512], axis=AX))
        ev = p.sig("vector", "vt", lambda v: v.reduce_max(colM[:, :], cmstrip[:, 0:NJT], axis=AX))
        p.wait("vector", ev)
        ev = p.sig("vector", "vt", lambda v: v.tensor_copy(colM_bf[:, :], colM[:, :]))
        p.wait("tensor", ev)
        ev = p.sig("tensor", "tm", lambda t: t.transpose(ps_tr[:1, :], colM_bf[:, :], id_sb[:, :]))
        p.wait("vector", ev)
        ev = p.sig("vector", "vt", lambda v: v.reduce_max(msr[:1, :], ps_tr[:1, :], axis=AX))
        p.wait("tensor", ev)
        ev = p.sig("tensor", "tm", lambda t: t.matmul(ps_bM, ones_r[:1, :], msr[:1, :], start=True, stop=True))
        p.wait("vector", ev)
        p.wait_now("vector", "sc")
        ev_negM = p.sig("vector", "vt", lambda v: v.tensor_scalar(negM[:, :], ps_bM, C(-1.0), None, op0=ALU.mult))

        for ic in range(2):
            gidx = h * 2 + ic
            for jt in range(NJT):
                g2 = gidx * NJT + jt
                k2 = jt % 2
                if jt >= 2:
                    p.wait("tensor", stt_evs[g2 - 2])
                if jt < 2:
                    p.wait_now("tensor", "vt")
                ev_mm = p.sig("tensor", "tm", lambda t, h=h, jt=jt, ic=ic, k2=k2: t.matmul(
                    ps_s[k2], khl(jt * 128, 128),
                    qT[:, h, ic * 512:(ic + 1) * 512], start=True, stop=True))
                p.wait("vector", ev_mm)
                if g2 >= 2:
                    p.wait("vector", exp_evs[g2 - 2])
                w0 = ic * 512 - jt * 128 + 3968
                stt_evs[g2] = p.sig("vector", "vt", lambda v, h=h, k2=k2, w0=w0: v.scalar_tensor_tensor(
                    sttv(k2), mabs(w0)[:, 0:512], negslope[:, h:h + 1], ps_s[k2],
                    op0=ALU.mult, op1=ALU.add))
                p.wait("scalar", stt_evs[g2])
                if jt == 0:
                    p.wait("scalar", ev_negM)
                if g2 >= 2:
                    p.wait("scalar", pv_evs[g2 - 2])
                exp_evs[g2] = p.sig("scalar", "sc", lambda s, k2=k2: s.activation(
                    out=prv(k2, 0, 512), in_=sttv(k2), func=ACTF.Exp, bias=negM[:, 0:1], scale=1.0))
                p.wait("tensor", exp_evs[g2])
                if jt == 0 and gidx >= 1:
                    for it in range(4):
                        p.wait("tensor", evac_evs[(gidx - 1, it)])

                def pv(t, h=h, jt=jt, k2=k2):
                    for it in range(4):
                        ins = t.matmul(ps_o[it], prv(k2, it * 128, 128),
                                       vhv(jt), start=(jt == 0), stop=(jt == NJT - 1))
                    return ins
                pv_evs[g2] = p.sig("tensor", "tm", pv)
            for it in range(4):
                gst = gidx * 4 + it
                p.wait("vector", pv_evs[gidx * NJT + NJT - 1])
                ev = p.sig("vector", "vt", lambda v, it=it: v.tensor_copy(scol[:, :], ps_o[it][:, DH:DH + 1]))
                p.wait("vector", ev)
                ev = p.sig("vector", "vt", lambda v: v.reciprocal(rscol[:, :], scol[:, :]))
                p.wait("vector", ev)
                if gst >= 2:
                    p.wait("vector", evst[gst - 2])
                ev2 = p.sig("vector", "vt", lambda v, it=it, gst=gst: v.tensor_tensor(
                    attn_scr[:, gst % 2, :], ps_o[it][:, 0:DH], rscol[:, 0:1].to_broadcast([128, DH]), op=ALU.mult))
                evac_evs[(gidx, it)] = ev2
                p.wait("sync", ev2)
                evst[gst] = p.dma("sync", "st", lambda s, h=h, ic=ic, it=it, gst=gst: s.dma_start(
                    out=attn_dr[ic * 512 + it * 128: ic * 512 + (it + 1) * 128, h * 128:(h + 1) * 128],
                    in_=attn_scr[:, gst % 2, :]))
        kv_consumed[h] = p.val("tm")
    p.play()

    attnT = xlnT
    wov = lambda ec, fc: bfv(ec * 1024 + fc * 512, [[1, 512]])
    ps_w = [psA[:, 0:1024], psA[:, 1024:2048]]
    p.wait_now("sync", "st")
    p.wait_now("sync", "tm")
    for ec in range(NEC):
        p.dma("sync", "ld", lambda s, ec=ec: s.dma_start(out=attnT[:, ec, :], in_=attn_dr[:, ec * 128:(ec + 1) * 128], transpose=True))
    for fh in range(2):
        p.wait_now("gpsimd", "tm")
        ev_w = p.dma("gpsimd", "gq", lambda g, fh=fh: g.dma_start(
            out=bfv(0, [[1, NEC * 1024]]), in_=AP(wo_d, fh * TLOC, [[NEC * HD, 128], [HD, NEC], [1, TLOC]])))
        p.wait("tensor", ev_w)
        p.wait_now("tensor", "ld")
        p.wait_now("tensor", "vt")
        xwo_ev = {}
        for tt in range(NT):
            if tt >= 2:
                p.wait("tensor", xwo_ev[tt - 2])

            def womm(t, tt=tt, fh=fh, ps=ps_w[tt % 2]):
                for ec in range(NEC):
                    for fc in range(2):
                        t.matmul(ps[:, fc * 512:(fc + 1) * 512], attnT[:, ec, tt * 128:(tt + 1) * 128],
                                 wov(ec, fc), start=(ec == 0), stop=False)
                for fc in range(2):
                    ins = t.matmul(ps[:, fc * 512:(fc + 1) * 512], ones_r[:1, 0:128],
                                   bias_bf["o"][:1, fh * TLOC + fc * 512:fh * TLOC + (fc + 1) * 512],
                                   start=False, stop=True)
                return ins
            ev_mm = p.sig("tensor", "tm", womm)
            p.wait("vector", ev_mm)

            def xadd(v, tt=tt, fh=fh, ps=ps_w[tt % 2]):
                for fc in range(2):
                    ins = v.tensor_add(x_sb[:, tt, fh * TLOC + fc * 512:fh * TLOC + (fc + 1) * 512],
                                       x_sb[:, tt, fh * TLOC + fc * 512:fh * TLOC + (fc + 1) * 512],
                                       ps[:, fc * 512:(fc + 1) * 512])
                return ins
            xwo_ev[tt] = p.sig("vector", "vt", xadd)
    p.wait_now("vector", "vt")
    p.play()

    emit_ln("c")
    emit_routing("c")
    emit_experts("c")

    p.wait_now("sync", "vt")
    for tt in range(NT):
        p.dma("sync", "st", lambda s, tt=tt: s.dma_start(out=out_d[tt * 128:(tt + 1) * 128, :], in_=x_sb[:, tt, :]))
    p.wait_now("sync", "st")
    p.play()

    ctx.close()
    return nc


def _prep_weights(inputs):
    def chunkT(w):
        wt = np.ascontiguousarray(w.T)
        return np.ascontiguousarray(wt.reshape(NEC, 128, HD).transpose(1, 0, 2).reshape(128, NEC * HD))

    d = {}
    d["wq"] = chunkT(inputs["L0_attn_Wq"])
    d["wk"] = chunkT(inputs["L0_attn_Wk"])
    d["wv"] = chunkT(inputs["L0_attn_Wv"])
    d["wo"] = chunkT(inputs["L0_attn_Wo"])
    d["bq"] = np.ascontiguousarray(inputs["L0_attn_bq"])
    d["bk"] = np.ascontiguousarray(inputs["L0_attn_bk"])
    d["bv"] = np.ascontiguousarray(inputs["L0_attn_bv"])
    d["bo"] = np.ascontiguousarray(inputs["L0_attn_bo"])
    d["slopes"] = np.ascontiguousarray(inputs["L0_attn_slopes"]).reshape(1, NH)
    d["gate_w"] = np.ascontiguousarray(inputs["L0_ffn_gate_w"].reshape(NEC, 128, E).transpose(1, 0, 2))
    d["wgate"] = np.ascontiguousarray(
        inputs["L0_ffn_w_gate"].reshape(E, NEC, 128, INTER).transpose(0, 2, 1, 3).reshape(E, 128, NEC * INTER))
    d["wup"] = np.ascontiguousarray(
        inputs["L0_ffn_w_up"].reshape(E, NEC, 128, INTER).transpose(0, 2, 1, 3).reshape(E, 128, NEC * INTER))
    d["wdown"] = np.ascontiguousarray(
        inputs["L0_ffn_w_down"].reshape(E, NIC, 128, HD).transpose(0, 2, 1, 3).reshape(E, 128, NIC * HD))
    d["tri"] = np.ascontiguousarray(np.tril(np.ones((128, 128), np.float32), -1).T).astype(ml_dtypes.bfloat16)
    d["ident"] = np.eye(128, dtype=np.float32).astype(ml_dtypes.bfloat16)
    return d


def kernel(L0_attn_x, L0_attn_Wq, L0_attn_bq, L0_attn_Wk, L0_attn_bk, L0_attn_Wv,
           L0_attn_bv, L0_attn_Wo, L0_attn_bo, L0_attn_slopes, L0_ffn_gate_w,
           L0_ffn_w_gate, L0_ffn_w_up, L0_ffn_w_down):
    inputs = dict(L0_attn_x=np.asarray(L0_attn_x), L0_attn_Wq=np.asarray(L0_attn_Wq),
                  L0_attn_bq=np.asarray(L0_attn_bq), L0_attn_Wk=np.asarray(L0_attn_Wk),
                  L0_attn_bk=np.asarray(L0_attn_bk), L0_attn_Wv=np.asarray(L0_attn_Wv),
                  L0_attn_bv=np.asarray(L0_attn_bv), L0_attn_Wo=np.asarray(L0_attn_Wo),
                  L0_attn_bo=np.asarray(L0_attn_bo), L0_attn_slopes=np.asarray(L0_attn_slopes),
                  L0_ffn_gate_w=np.asarray(L0_ffn_gate_w), L0_ffn_w_gate=np.asarray(L0_ffn_w_gate),
                  L0_ffn_w_up=np.asarray(L0_ffn_w_up), L0_ffn_w_down=np.asarray(L0_ffn_w_down))
    shared = _prep_weights(inputs)
    x = inputs["L0_attn_x"]

    nc = build_program()
    in_maps = []
    for c in range(NCORES):
        b, r = c // 4, c % 4
        m = dict(shared)
        m["x_in"] = np.ascontiguousarray(x[b, r * TLOC:(r + 1) * TLOC, :])
        qbase = r * TLOC
        w = np.arange(4992, dtype=np.float32)[None, :]
        q = np.arange(128, dtype=np.float32)[:, None]
        m["absm"] = np.ascontiguousarray(np.abs(w - 3968.0 + qbase - q))
        cm = np.zeros((8, 1), np.float32)
        cm[:c] = 1.0
        m["corem"] = cm
        in_maps.append(m)

    res = run_bass_kernel_spmd(nc, in_maps, core_ids=list(range(NCORES)))
    out = np.empty((B, T, HD), np.float32)
    for c in range(NCORES):
        b, r = c // 4, c % 4
        out[b, r * TLOC:(r + 1) * TLOC, :] = res.results[c]["out"]
    return out
